# version 11
# baseline (speedup 1.0000x reference)
import numpy as np
import ml_dtypes
import concourse.bass as bass
import concourse.mybir as mybir
from concourse.bass_utils import run_bass_kernel_spmd

F32 = mybir.dt.float32
BF16 = mybir.dt.bfloat16
ALU = mybir.AluOpType
ACT = mybir.ActivationFunctionType
AX = mybir.AxisListType

NCORES = 8
TOK = 8192
TPC = TOK // NCORES
D = 2048
HALF = 512
NT = TPC // 128
NHT = HALF // 128
IFF = 8192
NA = IFF // 128
INV_SQRT_D3 = float(np.float32(1.0 / np.sqrt(np.float32(128.0))))
SEARCH_BITS = [128.0, 64.0, 32.0, 16.0, 8.0, 4.0, 2.0, 1.0]


class Sem:
    __slots__ = ("h", "n")

    def __init__(self, nc, name):
        self.h = nc.alloc_semaphore(name)
        self.n = 0

    def inc(self, instr, k=1):
        instr.then_inc(self.h, k)
        self.n += k
        return self.n


def build_nc(debug=False, upto="C"):
    nc = bass.Bass("TRN2", target_bir_lowering=False)

    x_d = nc.dram_tensor("x", [TPC, D], F32, kind="ExternalInput").ap()
    wqkv_d = nc.dram_tensor("wqkv", [D, 6144], F32, kind="ExternalInput").ap()
    bqkv_d = nc.dram_tensor("bqkv", [128, 48], F32, kind="ExternalInput").ap()
    wout_d = nc.dram_tensor("wout", [D, D], F32, kind="ExternalInput").ap()
    bout_d = nc.dram_tensor("bout", [1, D], F32, kind="ExternalInput").ap()
    wg_d = nc.dram_tensor("wg", [IFF, D], F32, kind="ExternalInput").ap()
    wu_d = nc.dram_tensor("wu", [IFF, D], F32, kind="ExternalInput").ap()
    wd_d = nc.dram_tensor("wd", [D, IFF], F32, kind="ExternalInput").ap()
    identf_d = nc.dram_tensor("identf", [128, 128], F32, kind="ExternalInput").ap()
    h128_d = nc.dram_tensor("h128", [128, 128], F32, kind="ExternalInput").ap()
    mask4_d = nc.dram_tensor("mask4", [128, 512], F32, kind="ExternalInput").ap()
    ones_d = nc.dram_tensor("ones", [128, 128], F32, kind="ExternalInput").ap()
    out_d = nc.dram_tensor("out", [TPC, D], F32, kind="ExternalOutput").ap()
    dbg = {}
    if debug:
        for nm, shp, dt in [("dbg_xn", [TPC, D], F32), ("dbg_xq", [D, TPC], BF16),
                            ("dbg_rs", [128, 64], F32), ("dbg_sv", [128, 512], F32),
                            ("dbg_s", [TPC, D], F32), ("dbg_means", [128, 4], F32),
                            ("dbg_int", [128, IFF], F32), ("dbg_xq2", [128, IFF], F32),
                            ("dbg_sm", [128, 128], F32), ("dbg_s2", [TPC, D], F32)]:
            dbg[nm] = nc.dram_tensor(nm, shp, dt, kind="ExternalOutput").ap()

    s_sp = nc.dram_tensor("s_sp", [TPC, D], F32, kind="Internal").ap()
    rs_dram = nc.dram_tensor("rs_dram", [512, 16], F32, kind="Internal").ap()
    a1row_d = nc.dram_tensor("a1row", [1, TPC], F32, kind="Internal").ap()
    s2row_d = nc.dram_tensor("s2row", [1, HALF], F32, kind="Internal").ap()
    ternG_d = nc.dram_tensor("ternG", [NA * 16 * 128, 128], BF16, kind="Internal").ap()
    ternU_d = nc.dram_tensor("ternU", [NA * 16 * 128, 128], BF16, kind="Internal").ap()
    ternD_d = nc.dram_tensor("ternD", [IFF, D], BF16,
                             kind="ExternalOutput" if debug else "Internal").ap()

    LD = Sem(nc, "LD")
    G = Sem(nc, "G")
    Q = Sem(nc, "Q")
    TC = Sem(nc, "TC")
    VC = Sem(nc, "VC")
    SC = Sem(nc, "SC")
    GC = Sem(nc, "GC")
    SX = [Sem(nc, "SX0"), Sem(nc, "SX1")]
    SW = [Sem(nc, "SW0"), Sem(nc, "SW1"), Sem(nc, "SW2")]
    SBB = Sem(nc, "SBB")
    SSL = [Sem(nc, "SSL0"), Sem(nc, "SSL1")]
    SWT = [Sem(nc, "SWTa"), Sem(nc, "SWTb")]
    SWD = [Sem(nc, "SWD0"), Sem(nc, "SWD1")]
    SDQ = Sem(nc, "SDQ")
    SCMB = [Sem(nc, "SCMB0"), Sem(nc, "SCMB1")]
    QM = [Sem(nc, "QM%d" % i) for i in range(8)]
    GS = [Sem(nc, "GS0"), Sem(nc, "GS1")]
    GT = Sem(nc, "GT")
    GW = Sem(nc, "GW")
    GR = Sem(nc, "GR")
    GA = Sem(nc, "GA")
    GOUT = Sem(nc, "GOUT")
    ALL_SEMS = [LD, G, Q, TC, VC, SC, GC, SBB, SDQ, GT, GW, GR, GA, GOUT] + SX + SW + SSL + SWD + QM + GS + SWT + SCMB

    A = lambda name, shape, dt: nc.alloc_sbuf_tensor("sb_" + name, shape, dt).ap()
    identf = A("identf", [128, 128], F32)
    h128 = A("h128", [128, 128], F32)
    ones = A("ones", [128, 128], F32)
    amax1 = A("amax1", [128, NT], F32)
    sm = A("sm", [128, 16], F32)
    msums = A("msums", [128, 4], F32)
    msw = A("msw", [128, 768], F32)
    wmean = A("wmean", [128, 4], F32)
    wspos = A("wspos", [128, 4], F32)
    wsposb = A("wsposb", [128, 4], F32)
    wsneg = A("wsneg", [128, 4], F32)
    wmb = A("wmb", [128, 4], F32)
    m4 = A("m4", [128, 4], F32)
    m4n = A("m4n", [128, 4], F32)
    s4 = A("s4", [128, 4], F32)
    amax2 = A("amax2", [128, 4], F32)
    deq2 = A("deq2", [128, 4], F32)
    bm05 = A("bm05", [128, 1], F32)
    bp05 = A("bp05", [128, 1], F32)
    xqT = A("xqT", [128, 16 * TPC], BF16)

    pp = [nc.alloc_psum_tensor("pp%d" % i, [128, 512], F32).ap() for i in range(8)]

    sync, tensor, vector, scalar, gpsimd = nc.sync, nc.tensor, nc.vector, nc.scalar, nc.gpsimd

    def ld(dst, src):
        LD.inc(sync.dma_start(out=dst, in_=src), 16)
        return LD.n

    def gst(dst, src):
        G.inc(gpsimd.dma_start(out=dst, in_=src), 16)
        return G.n

    def lds(sem, dst, src2, **kw):
        sem.inc(sync.dma_start(out=dst, in_=src2, **kw), 16)
        return sem.n

    def gsts(sem, dst, src2, **kw):
        sem.inc(gpsimd.dma_start(out=dst, in_=src2, **kw), 16)
        return sem.n

    def barrier():
        VC.inc(vector.engine_nop())
        SC.inc(scalar.copy(sm[:, 15:16], sm[:, 15:16]))
        for eng in (sync, tensor, vector, scalar, gpsimd):
            for s in ALL_SEMS:
                if s.n:
                    eng.wait_ge(s.h, s.n)

    ld(identf, identf_d)
    ld(h128, h128_d)
    ld(ones, ones_d)
    vector.wait_ge(LD.h, LD.n)
    vector.memset(sm[:, 5:6], 1e-5)
    vector.memset(bm05, -0.5)
    vector.memset(bp05, 0.5)
    VC.inc(vector.memset(msums, 0.0))

    mch_t = nc.sbuf_tensor("mchunk", [128, 3072], F32)
    mchunk = mch_t.__enter__().ap()
    means_jobs = []
    for mi, wsrc in enumerate([wg_d, wu_d, wd_d]):
        flat = wsrc.rearrange("a b -> (a b)").rearrange("(p n) -> p n", p=128)
        for j in range(256):
            means_jobs.append((mi, flat[:, j * 512:(j + 1) * 512]))
    mst = {"issued": 0, "done": 0}
    means_abs = {}
    means_marks = {}

    def means_issue():
        i = mst["issued"]
        if i >= len(means_jobs):
            return
        _, src = means_jobs[i]
        buf = mchunk[:, (i % 6) * 512:((i % 6) + 1) * 512]
        if i >= 6:
            scalar.wait_ge(SC.h, means_abs[i - 6])
        means_marks[i] = QM[i % 6].inc(scalar.dma_start(out=buf, in_=src), 16)
        mst["issued"] = i + 1

    def means_step(k=1):
        for _ in range(k):
            i = mst["done"]
            if i >= len(means_jobs):
                return
            while mst["issued"] < min(i + 6, len(means_jobs)):
                means_issue()
            mi, _ = means_jobs[i]
            buf = mchunk[:, (i % 6) * 512:((i % 6) + 1) * 512]
            scalar.wait_ge(QM[i % 6].h, means_marks[i])
            means_abs[i] = SC.inc(
                scalar.activation(buf, buf, ACT.Abs, accum_out=msw[:, i:i + 1]))
            mst["done"] = i + 1

    xnT_t = nc.sbuf_tensor("xnT", [128, 16 * TPC], F32)
    xnT = xnT_t.__enter__().ap()
    workA_t = nc.sbuf_tensor("workA", [128, 4 * D], F32)
    workA = workA_t.__enter__().ap()

    xb = [workA[:, 0:D], workA[:, D:2 * D]]
    xnb = workA[:, 2 * D:3 * D]
    mkb = workA[:, 3 * D:4 * D]

    vc_tile = [0, 0]
    evac_mark = {}
    for i in range(NT):
        b = i % 2
        if vc_tile[b]:
            sync.wait_ge(VC.h, vc_tile[b])
        lv = lds(SX[b], xb[b], x_d[i * 128:(i + 1) * 128, :])
        vector.wait_ge(SX[b].h, lv)
        VC.inc(vector.tensor_reduce(sm[:, 0:1], xb[b], AX.X, ALU.add))
        gpsimd.wait_ge(VC.h, VC.n)
        GC.inc(gpsimd.tensor_scalar(sm[:, 1:2], sm[:, 0:1], -1.0 / 2048.0, None, ALU.mult))
        scalar.wait_ge(GC.h, GC.n)
        SC.inc(scalar.activation(xnb, xb[b], ACT.Identity, bias=sm[:, 1:2], scale=1.0))
        vector.wait_ge(SC.h, SC.n)
        vector.tensor_tensor(mkb, xnb, xnb, ALU.mult)
        VC.inc(vector.tensor_reduce(sm[:, 2:3], mkb, AX.X, ALU.add))
        scalar.wait_ge(VC.h, VC.n)
        SC.inc(scalar.activation(sm[:, 3:4], sm[:, 2:3], ACT.Sqrt, bias=sm[:, 5:6],
                                 scale=1.0 / 2048.0))
        vector.wait_ge(SC.h, SC.n)
        VC.inc(vector.reciprocal(sm[:, 4:5], sm[:, 3:4]))
        gpsimd.wait_ge(VC.h, VC.n)
        GC.inc(gpsimd.tensor_copy(sm[:, 12:13], sm[:, 4:5]))
        vector.wait_ge(GC.h, GC.n)
        vector.tensor_scalar(xnb, xnb, sm[:, 12:13], None, ALU.mult)
        VC.inc(vector.tensor_reduce(sm[:, 13:14], xnb, AX.X, ALU.max,
                                    apply_absolute_value=True))
        gpsimd.wait_ge(VC.h, VC.n)
        GC.inc(gpsimd.tensor_scalar(amax1[:, i:i + 1], sm[:, 13:14], 1e-5, None, ALU.max))
        vector.wait_ge(GC.h, GC.n)
        VC.inc(vector.reciprocal(sm[:, 6:7], amax1[:, i:i + 1]))
        scalar.wait_ge(VC.h, VC.n)
        SC.inc(scalar.copy(sm[:, 7:8], sm[:, 6:7]))
        if debug:
            gpsimd.wait_ge(SC.h, SC.n)
            gdm = gst(dbg["dbg_sm"][:, i * 16:(i + 1) * 16], sm)
            vector.wait_ge(G.h, gdm)
        tensor.wait_ge(VC.h, VC.n)
        for k4 in range(4):
            bank = k4 % 2
            if bank in evac_mark:
                tensor.wait_ge(VC.h, evac_mark[bank])
            for kk in range(4):
                k = k4 * 4 + kk
                mm = tensor.transpose(pp[bank][:, kk * 128:(kk + 1) * 128],
                                      xnb[:, k * 128:(k + 1) * 128], identf)
            TC.inc(mm)
            vector.wait_ge(TC.h, TC.n)
            dst = xnT.rearrange("p (k t) -> p k t", t=TPC)[:, k4 * 4:(k4 + 1) * 4,
                                                          i * 128:(i + 1) * 128]
            evac_mark[bank] = VC.inc(
                vector.tensor_copy(dst, pp[bank].rearrange("p (k t) -> p k t", t=128)))
        if debug:
            gpsimd.wait_ge(VC.h, VC.n)
            gm = gst(dbg["dbg_xn"][i * 128:(i + 1) * 128, :], xnb)
            vector.wait_ge(G.h, gm)
        u, dd = xb[b], xnb
        vector.wait_ge(SC.h, SC.n)
        vector.tensor_scalar(u, dd, sm[:, 7:8], 128.0, ALU.mult, ALU.mult)
        vector.tensor_scalar(u, u, 128.5, 255.4999, ALU.add, ALU.min)
        vector.tensor_copy(dd, u)
        for bit in SEARCH_BITS:
            vector.tensor_scalar(mkb, dd, bit, -bit, ALU.is_ge, ALU.mult)
            vector.tensor_tensor(dd, dd, mkb, ALU.add)
        VC.inc(vector.scalar_tensor_tensor(mkb, u, 128.0, dd, ALU.subtract, ALU.subtract))
        vc_tile[b] = VC.n
        tensor.wait_ge(VC.h, VC.n)
        for k4 in range(4):
            bank = 2 + (k4 % 2)
            if bank in evac_mark:
                tensor.wait_ge(VC.h, evac_mark[bank])
            for kk in range(4):
                k = k4 * 4 + kk
                mm = tensor.transpose(pp[bank][:, kk * 128:(kk + 1) * 128],
                                      mkb[:, k * 128:(k + 1) * 128], identf)
            TC.inc(mm)
            vector.wait_ge(TC.h, TC.n)
            dstq = xqT.rearrange("p (k t) -> p k t", t=TPC)[:, k4 * 4:(k4 + 1) * 4,
                                                           i * 128:(i + 1) * 128]
            evac_mark[bank] = VC.inc(
                vector.tensor_copy(dstq, pp[bank].rearrange("p (k t) -> p k t", t=128)))
        means_step(8)

    if debug:
        gpsimd.wait_ge(VC.h, VC.n)
        for k in range(16):
            gst(dbg["dbg_xq"][k * 128:(k + 1) * 128, :], xqT[:, k * TPC:(k + 1) * TPC])

    gpsimd.wait_ge(GC.h, GC.n)
    GA.inc(gpsimd.dma_start(out=a1row_d.rearrange("o (c p) -> o p c", p=128)[0],
                            in_=amax1, allow_slow_non_contiguous=True), 16)

    if upto == "A":
        barrier()
        return nc, dbg

    workA_t.__exit__(None, None, None)
    wqs_t = nc.sbuf_tensor("wqs", [128, 3 * 1024], F32)
    wqs = wqs_t.__enter__().ap()
    qkT_t = nc.sbuf_tensor("qkT", [128, 32 * HALF], F32)
    qkT = qkT_t.__enter__().ap()
    workB_t = nc.sbuf_tensor("workB", [128, 1024], F32)
    workB = workB_t.__enter__().ap()
    bsm_t = nc.sbuf_tensor("bsm", [128, 2048 + 1712], F32)
    bsm = bsm_t.__enter__().ap()
    rs_all = bsm[:, 0:64]
    rs_t = bsm[:, 64:128]
    sall = bsm[:, 1536 + 256:1536 + 320]
    sallb = bsm[:, 1536 + 320:1536 + 384]
    rall = bsm[:, 1536 + 384:1536 + 448]
    rallb = bsm[:, 1536 + 448:1536 + 512]
    mask4 = bsm[:, 2048:2560]
    scb = bsm[:, 2560:3584]
    t1sb = bsm[:, 3584:3712]
    bqkv = bsm[:, 3712:3760]
    ld(mask4, mask4_d)
    ld(bqkv, bqkv_d)
    svT = bsm[:, 128:128 + HALF]
    sv_t = bsm[:, 128 + HALF:128 + 2 * HALF]
    bb = bsm[:, 128 + 2 * HALF:128 + 2 * HALF + 512]
    attn_s = qkT[:, 8192:8192 + D]
    sst = [workB[:, 0:512], workB[:, 512:1024]]


    wq_r = wqkv_d.rearrange("(k p) n -> p k n", p=128)
    barrier()

    for hf in range(2):
        t0 = hf * HALF
        tensor.wait_ge(VC.h, VC.n)
        tensor.wait_ge(SC.h, SC.n)
        sync.wait_ge(TC.h, TC.n)
        VC.inc(vector.memset(svT, 0.0))
        evac_q = {0: SC.n, 1: SC.n}
        evac_v = {0: VC.n, 1: VC.n}
        mm_done = {0: TC.n, 1: TC.n}
        slot_mm = {0: TC.n, 1: TC.n, 2: TC.n}
        for c in range(48):
            b = c % 2
            if c >= 2:
                tensor.wait_ge(SC.h, evac_q[b])
                tensor.wait_ge(VC.h, evac_v[b])
            for hs in range(2):
                s4i = (2 * c + hs) % 3
                sync.wait_ge(TC.h, slot_mm[s4i])
                dstw = wqs[:, s4i * 1024:(s4i + 1) * 1024].rearrange("p (k c) -> p k c", c=128)
                lv = lds(SW[s4i], dstw, wq_r[:, 8 * hs:8 * hs + 8, c * 128:(c + 1) * 128])
                tensor.wait_ge(SW[s4i].h, lv)
                for kk in range(8):
                    k = 8 * hs + kk
                    mm = tensor.matmul(pp[b],
                                       wqs[:, s4i * 1024 + kk * 128: s4i * 1024 + (kk + 1) * 128],
                                       xnT[:, k * TPC + t0: k * TPC + t0 + HALF],
                                       start=(k == 0), stop=(k == 15))
                slot_mm[s4i] = TC.inc(mm)
            mm_done[b] = slot_mm[(2 * c + 1) % 3]
            h = c // 3
            part = c % 3
            if part < 2:
                dstqk = qkT.rearrange("p (r t h2) -> p r t h2", r=2, h2=16)[:, part, :, h]
                scalar.wait_ge(TC.h, TC.n)
                evac_q[b] = SC.inc(
                    scalar.activation(dstqk, pp[b],
                                      ACT.Identity, bias=bqkv[:, c:c + 1], scale=1.0))
            else:
                vector.wait_ge(TC.h, TC.n)
                evac_v[b] = VC.inc(
                    vector.scalar_tensor_tensor(svT, pp[b], bqkv[:, c:c + 1], svT,
                                                ALU.add, ALU.add))
            means_step(5)
        if upto == "B1":
            break
        scalar.wait_ge(VC.h, VC.n)
        sc_bank_free = {2: VC.n, 3: VC.n}
        for gb in range(16):
            bank = 2 + gb % 2
            if gb == 0:
                tensor.wait_ge(SC.h, SC.n)
            tensor.wait_ge(VC.h, sc_bank_free[bank])
            for q4 in range(4):
                g = gb * 4 + q4
                qap = qkT[:, g * 128:(g + 1) * 128]
                kap = qkT[:, 8192 + g * 128: 8192 + (g + 1) * 128]
                mm = tensor.matmul(pp[bank][:, q4 * 128:(q4 + 1) * 128], qap, kap,
                                   start=True, stop=True)
            TC.inc(mm)
            vector.wait_ge(TC.h, TC.n)
            vector.scalar_tensor_tensor(scb[:, 0:512], pp[bank], INV_SQRT_D3, mask4,
                                        ALU.mult, ALU.add)
            sc_bank_free[bank] = VC.inc(vector.engine_nop())
            VC.inc(vector.tensor_reduce(m4, scb[:, 0:512].rearrange("p (g n) -> p g n", n=128),
                                         AX.X, ALU.max))
            gpsimd.wait_ge(VC.h, VC.n)
            GC.inc(gpsimd.tensor_scalar(m4n, m4, -1.0, None, ALU.mult))
            scalar.wait_ge(GC.h, GC.n)
            for q4 in range(4):
                mm2 = scalar.activation(scb[:, 512 + q4 * 128: 512 + (q4 + 1) * 128],
                                        scb[:, q4 * 128:(q4 + 1) * 128], ACT.Exp,
                                        bias=m4n[:, q4:q4 + 1], scale=1.0)
            SC.inc(mm2)
            vector.wait_ge(SC.h, SC.n)
            VC.inc(vector.tensor_reduce(sall[:, gb * 4:(gb + 1) * 4],
                                        scb[:, 512:1024].rearrange("p (g n) -> p g n", n=128),
                                        AX.X, ALU.add))
            means_step(2)
        if upto == "B2":
            break
        gpsimd.wait_ge(VC.h, VC.n)
        GC.inc(gpsimd.tensor_copy(sallb, sall))
        vector.wait_ge(GC.h, GC.n)
        VC.inc(vector.reciprocal(rall, sallb))
        scalar.wait_ge(VC.h, VC.n)
        SC.inc(scalar.copy(rallb, rall))
        vector.wait_ge(SC.h, SC.n)
        VC.inc(vector.tensor_tensor(rs_all[:, 0:64], sallb, rallb, ALU.mult))
        tensor.wait_ge(VC.h, VC.n)
        TC.inc(tensor.transpose(pp[4][0:64, 0:128], rs_all[:, 0:64], identf))
        vector.wait_ge(TC.h, TC.n)
        VC.inc(vector.tensor_copy(t1sb[0:64, :], pp[4][0:64, 0:128]))
        gpsimd.wait_ge(VC.h, VC.n)
        g1 = gsts(GR, rs_dram.rearrange("(g r) i -> g (r i)", r=8), t1sb[0:64, 0:128])
        gpsimd.wait_ge(GR.h, g1)
        rs_sh = gsts(GR, rs_t[:, 0:64].rearrange("p (j i) -> p j i", i=16),
                     rs_dram.rearrange("(j p) i -> p j i", p=128))
        for j in range(NHT):
            mm = tensor.transpose(pp[5][:, j * 128:(j + 1) * 128],
                                  svT[:, j * 128:(j + 1) * 128], identf)
        TC.inc(mm)
        vector.wait_ge(TC.h, TC.n)
        VC.inc(vector.tensor_copy(sv_t, pp[5]))
        if debug and hf == 0:
            gpsimd.wait_ge(VC.h, VC.n)
            gpsimd.wait_ge(G.h, rs_sh)
            gst(dbg["dbg_rs"], rs_t)
            gst(dbg["dbg_sv"], sv_t)
        if upto == "B3":
            break
        attnT = qkT[:, 0:16 * HALF]
        vector.wait_ge(GR.h, rs_sh)
        for j in range(NHT):
            for i in range(16):
                vector.tensor_scalar(attn_s[:, i * 128:(i + 1) * 128],
                                     sv_t[:, j * 128:(j + 1) * 128],
                                     rs_t[:, j * 16 + i:j * 16 + i + 1], None, ALU.mult)
            VC.inc(vector.engine_nop())
            tensor.wait_ge(VC.h, VC.n)
            for c4 in range(4):
                bank = 6 + (c4 % 2)
                if bank in evac_mark:
                    tensor.wait_ge(VC.h, evac_mark[bank])
                for cc in range(4):
                    c = c4 * 4 + cc
                    mm = tensor.transpose(pp[bank][:, cc * 128:(cc + 1) * 128],
                                          attn_s[:, c * 128:(c + 1) * 128], identf)
                TC.inc(mm)
                vector.wait_ge(TC.h, TC.n)
                dsta = attnT.rearrange("p (c t) -> p c t", t=HALF)[:, c4 * 4:(c4 + 1) * 4,
                                                                  j * 128:(j + 1) * 128]
                evac_mark[bank] = VC.inc(
                    vector.tensor_copy(dsta, pp[bank].rearrange("p (c t) -> p c t", t=128)))
            means_step(6)
        if upto == "B5":
            break
        s_done = [0, 0]
        bb_free = 0
        wout_mm = {0: TC.n, 1: TC.n}
        evac_vc = VC.n
        for mb in range(4):
            if bb_free:
                sync.wait_ge(VC.h, bb_free)
            bb_done = lds(SBB, bb, bout_d[:, mb * 512:(mb + 1) * 512].broadcast_to((128, 512)))
            tensor.wait_ge(VC.h, evac_vc)
            if mb == 0:
                tensor.wait_ge(SC.h, SC.n)
            for hd in range(16):
                b = hd % 2
                sync.wait_ge(TC.h, wout_mm[b])
                lv = lds(SW[b], wqs[:, b * 1024: b * 1024 + 512],
                         wout_d[hd * 128:(hd + 1) * 128, mb * 512:(mb + 1) * 512])
                tensor.wait_ge(SW[b].h, lv)
                for j in range(NHT):
                    mm = tensor.matmul(pp[j],
                                       attnT[:, hd * HALF + j * 128: hd * HALF + (j + 1) * 128],
                                       wqs[:, b * 1024: b * 1024 + 512],
                                       start=(hd == 0), stop=(hd == 15))
                wout_mm[b] = TC.inc(mm)
            vector.wait_ge(TC.h, TC.n)
            vector.wait_ge(SBB.h, bb_done)
            for j in range(NHT):
                bs = j % 2
                if s_done[bs]:
                    vector.wait_ge(GS[bs].h, s_done[bs])
                bb_free = VC.inc(vector.tensor_tensor(sst[bs], pp[j], bb, ALU.add))
                gpsimd.wait_ge(VC.h, VC.n)
                s_done[bs] = gsts(
                    GS[bs],
                    s_sp[t0 + j * 128: t0 + (j + 1) * 128, mb * 512:(mb + 1) * 512],
                    sst[bs])
                if debug:
                    s_done[bs] = gsts(
                        GS[bs],
                        dbg["dbg_s"][t0 + j * 128: t0 + (j + 1) * 128,
                                     mb * 512:(mb + 1) * 512], sst[bs])
            evac_vc = VC.n
            means_step(8)
    means_step(len(means_jobs))

    barrier()
    bsm_t.__exit__(None, None, None)
    workB_t.__exit__(None, None, None)
    qkT_t.__exit__(None, None, None)
    wqs_t.__exit__(None, None, None)
    xnT_t.__exit__(None, None, None)
    mch_t.__exit__(None, None, None)

    if upto.startswith("B"):
        return nc, dbg

    vector.wait_ge(SC.h, SC.n)
    vector.memset(msums, 0.0)
    for mi3 in range(3):
        vector.tensor_reduce(msums[:, mi3:mi3 + 1], msw[:, mi3 * 256:(mi3 + 1) * 256],
                             AX.X, ALU.add)
    VC.inc(vector.engine_nop())
    tensor.wait_ge(VC.h, VC.n)
    TC.inc(tensor.matmul(pp[4][:, 0:4], ones, msums, start=True, stop=True))
    vector.wait_ge(TC.h, TC.n)
    VC.inc(vector.tensor_scalar(wmean[:, 0:4], pp[4][:, 0:4], 1.0 / 16777216.0, 1e-5,
                                ALU.mult, ALU.max))
    gpsimd.wait_ge(VC.h, VC.n)
    GC.inc(gpsimd.tensor_copy(wmb[:, 0:4], wmean[:, 0:4]))
    vector.wait_ge(GC.h, GC.n)
    VC.inc(vector.reciprocal(wspos[:, 0:4], wmb[:, 0:4]))
    gpsimd.wait_ge(VC.h, VC.n)
    GC.inc(gpsimd.tensor_scalar(wsneg[:, 0:4], wspos[:, 0:4], -1.0, None, ALU.mult))
    scalar.wait_ge(VC.h, VC.n)
    SC.inc(scalar.copy(wsposb[:, 0:4], wspos[:, 0:4]))
    if debug:
        gpsimd.wait_ge(VC.h, VC.n)
        gst(dbg["dbg_means"], wmb)

    inter_t = nc.sbuf_tensor("inter", [128, NA * HALF], F32)
    inter = inter_t.__enter__().ap()
    cbuf_t = nc.sbuf_tensor("cbuf", [128, 2 * 1024 + 1024 + 2048], F32)
    cbuf = cbuf_t.__enter__().ap()
    wt_t = nc.sbuf_tensor("wt", [128, 8 * 1024], BF16)
    wtb = wt_t.__enter__().ap()
    dq_t = nc.sbuf_tensor("dq", [128, 3 * 512], F32)
    dqb = dq_t.__enter__().ap()

    slab = [cbuf[:, i * 1024:(i + 1) * 1024] for i in range(2)]
    s2bb = cbuf[:, 2048:2560]
    mmaxb = cbuf[:, 2560:3072]
    pnb = cbuf[:, 3072:5120]
    wtg2 = [wtb[:, 0:2048], wtb[:, 2048:4096]]
    wtu2 = [wtb[:, 4096:6144], wtb[:, 6144:8192]]
    deqg = dqb[:, 0:512]
    dequ = dqb[:, 512:1024]
    gt1 = dqb[:, 1024:1536]
    gt2 = deqg
    gt3 = dequ

    wg_r = wg_d.rearrange("(a p) h -> a p h", p=128)
    wu_r = wu_d.rearrange("(a p) h -> a p h", p=128)
    ternG_r = ternG_d.rearrange("(a k p) o -> a p k o", k=16, p=128)
    ternU_r = ternU_d.rearrange("(a k p) o -> a p k o", k=16, p=128)
    barrier()

    for hf in range(2):
        t0 = hf * HALF
        if hf == 1:
            for eng in (vector, scalar, gpsimd, tensor, sync):
                eng.wait_ge(GOUT.h, GOUT.n)
                eng.wait_ge(SCMB[0].h, SCMB[0].n)
                eng.wait_ge(SCMB[1].h, SCMB[1].n)
        lv = lds(SDQ, deqg, a1row_d[:, t0:t0 + HALF].broadcast_to((128, HALF)))
        lv = lds(SDQ, dequ, a1row_d[:, t0:t0 + HALF].broadcast_to((128, HALF)))
        vector.wait_ge(SDQ.h, lv)
        vector.tensor_scalar(deqg, deqg, wmb[:, 0:1], 1.0 / 256.0, ALU.mult, ALU.mult)
        VC.inc(vector.tensor_scalar(dequ, dequ, wmb[:, 1:2], 1.0 / 256.0,
                                    ALU.mult, ALU.mult))

        slab_free = {i: GC.n for i in range(2)}
        h128ev = {6: VC.n, 7: VC.n}
        pnh = [pnb[:, 0:1024], pnb[:, 1024:2048]]
        pn_free = {0: TC.n, 1: TC.n}
        hsc_state = {"c": 0}
        wt_free = {0: TC.n, 1: TC.n}
        bank_free = {0: VC.n, 1: VC.n, 2: VC.n, 3: VC.n}
        tern_spill = {0: G.n, 1: G.n}
        wt_ready = {}
        gu_mm = {}
        inter_done = {}
        sil_mark = {}
        c1silu = cbuf[:, 2048:2560]
        c1up = cbuf[:, 2560:3072]

        def c1_produce(a):
            ab = a % 2
            wtg = wtg2[ab]
            wtu = wtu2[ab]
            if hf == 0:
                groups = []
                for mi2, (wsrc, tdst, wtdst, scol) in enumerate(
                        [(wg_r, ternG_r, wtg, 0), (wu_r, ternU_r, wtu, 1)]):
                    for hs in range(2):
                        groups.append((mi2, wsrc, wtdst, scol, hs))

                def g_tern(gi):
                    mi2, wsrc, wtdst, scol, hs = groups[gi]
                    hc = hsc_state["c"] % 2
                    hsc_state["c"] += 1
                    sb = slab[hs]
                    sync.wait_ge(GC.h, slab_free[hs])
                    lv = lds(SSL[hs], sb, wsrc[a, :, hs * 1024:(hs + 1) * 1024])
                    scalar.wait_ge(SSL[hs].h, lv)
                    scalar.wait_ge(TC.h, pn_free[hc])
                    scalar.activation(pnh[hc], sb, ACT.Sign,
                                      scale=wspos[:, scol:scol + 1], bias=bm05[:, 0:1])
                    smark = SC.inc(scalar.activation(
                        sb, sb, ACT.Sign,
                        scale=wspos[:, scol:scol + 1], bias=bp05[:, 0:1]))
                    gpsimd.wait_ge(SC.h, smark)
                    slab_free[hs] = GC.inc(
                        gpsimd.tensor_tensor(pnh[hc], pnh[hc], sb, ALU.add))
                    tensor.wait_ge(GC.h, slab_free[hs])
                    marks = []
                    for k8 in range(2):
                        bank = 4 + (gi % 2) * 2 + k8
                        if bank in evac_mark:
                            tensor.wait_ge(VC.h, evac_mark[bank])
                        if bank in h128ev:
                            tensor.wait_ge(VC.h, h128ev[bank])
                        for kk in range(4):
                            k = k8 * 4 + kk
                            mm = tensor.transpose(pp[bank][:, kk * 128:(kk + 1) * 128],
                                                  pnh[hc][:, k * 128:(k + 1) * 128],
                                                  identf)
                        marks.append(TC.inc(mm))
                    pn_free[hc] = marks[-1]
                    return marks

                def g_evac(gi, marks):
                    mi2, wsrc, wtdst, scol, hs = groups[gi]
                    if a >= 2 and gi == 0:
                        vector.wait_ge(TC.h, wt_free[ab])
                        vector.wait_ge(GT.h, tern_spill[ab])
                    for k8 in range(2):
                        bank = 4 + (gi % 2) * 2 + k8
                        vector.wait_ge(TC.h, marks[k8])
                        kb = hs * 8 + k8 * 4
                        dstw = wtdst[:, 0:2048].rearrange(
                            "p (k o) -> p k o", o=128)[:, kb:kb + 4, :]
                        evac_mark[bank] = VC.inc(vector.tensor_copy(
                            dstw, pp[bank].rearrange("p (k o) -> p k o", o=128)))
                    if hs == 1:
                        tdst = [ternG_r, ternU_r][mi2]
                        wtd = [wtg, wtu][mi2]
                        gpsimd.wait_ge(VC.h, VC.n)
                        tern_spill[ab] = gsts(
                            GT, tdst[a],
                            wtd[:, 0:2048].rearrange("p (k o) -> p k o", o=128))

                if a >= 2:
                    sync.wait_ge(TC.h, wt_free[ab])
                    sync.wait_ge(GT.h, tern_spill[ab])
                pend = None
                for gi in range(4):
                    marks = g_tern(gi)
                    if pend is not None:
                        g_evac(pend[0], pend[1])
                    pend = (gi, marks)
                g_evac(pend[0], pend[1])
                wt_ready[a] = VC.n
            else:
                if a >= 2:
                    sync.wait_ge(TC.h, wt_free[ab])
                lds(SWT[ab], wtg.rearrange("p (k o) -> p k o", o=128), ternG_r[a])
                wt_ready[a] = lds(SWT[ab], wtu.rearrange("p (k o) -> p k o", o=128),
                                  ternU_r[a])

        def c1_consume(a):
            ab = a % 2
            wtg = wtg2[ab]
            wtu = wtu2[ab]
            if hf == 0:
                tensor.wait_ge(VC.h, wt_ready[a])
            else:
                tensor.wait_ge(SWT[ab].h, wt_ready[a])
            bg, bu = ab, 2 + ab
            tensor.wait_ge(VC.h, bank_free[bg])
            tensor.wait_ge(VC.h, bank_free[bu])
            for k in range(16):
                mm = tensor.matmul(pp[bg],
                                   wtg[:, k * 128:(k + 1) * 128],
                                   xqT[:, k * TPC + t0: k * TPC + t0 + HALF],
                                   start=(k == 0), stop=(k == 15))
            TC.inc(mm)
            for k in range(16):
                mm = tensor.matmul(pp[bu],
                                   wtu[:, k * 128:(k + 1) * 128],
                                   xqT[:, k * TPC + t0: k * TPC + t0 + HALF],
                                   start=(k == 0), stop=(k == 15))
            wt_free[ab] = gu_mm[a] = TC.inc(mm)

        def c1_post(a):
            ab = a % 2
            bg, bu = ab, 2 + ab
            vector.wait_ge(TC.h, gu_mm[a])
            if a >= 1:
                vector.wait_ge(SC.h, sil_mark[a - 1])
            vector.tensor_tensor(gt1, pp[bg], deqg, ALU.mult)
            VC.inc(vector.engine_nop())
            scalar.wait_ge(VC.h, VC.n)
            if a >= 1:
                scalar.wait_ge(VC.h, inter_done[a - 1])
            sil_mark[a] = SC.inc(scalar.activation(c1silu, gt1, ACT.Silu))
            vector.tensor_tensor(c1up, pp[bu], dequ, ALU.mult)
            vector.wait_ge(SC.h, sil_mark[a])
            bank_free[bg] = bank_free[bu] = inter_done[a] = VC.inc(
                vector.tensor_tensor(inter[:, a * HALF:(a + 1) * HALF], c1silu, c1up,
                                     ALU.mult))

        def c1_h128(a):
            bh = 6 + a % 2
            tensor.wait_ge(VC.h, inter_done[a])
            tensor.wait_ge(VC.h, h128ev[bh])
            if hf == 0 and bh in evac_mark:
                tensor.wait_ge(VC.h, evac_mark[bh])
            h_mm = TC.inc(tensor.matmul(pp[bh], h128, inter[:, a * HALF:(a + 1) * HALF],
                                        start=True, stop=True))
            vector.wait_ge(TC.h, h_mm)
            h128ev[bh] = VC.inc(
                vector.tensor_copy(inter[:, a * HALF:(a + 1) * HALF], pp[bh]))

        for ci in range(NA + 3):
            if ci < NA:
                c1_produce(ci)
            if 1 <= ci <= NA:
                c1_consume(ci - 1)
            if 2 <= ci <= NA + 1:
                c1_post(ci - 2)
            if 3 <= ci <= NA + 2:
                c1_h128(ci - 3)
        if hf == 0:
            wd_r = wd_d.rearrange("(r p) o -> r p o", p=128)
            ternD_w = ternD_d.rearrange("(og t p) m -> og p t m", t=8, p=128)
            wd_slab_free = {0: GC.n, 1: GC.n}
            wd_pn_free = {0: TC.n, 1: TC.n}
            wd_stage_free = {0: GW.n, 1: GW.n}
            wd_evmk = {}
            wd_jobs = [(r, og) for r in range(16) for og in range(8)]

            def wd_tern(i):
                r, og = wd_jobs[i]
                sidx = i % 2
                sb = slab[sidx]
                if i < 2:
                    sync.wait_ge(GC.h, slab_free[sidx])
                else:
                    sync.wait_ge(GC.h, wd_slab_free[sidx])
                lv = lds(SSL[sidx], sb, wd_r[r, :, og * 1024:(og + 1) * 1024])
                scalar.wait_ge(SSL[sidx].h, lv)
                scalar.wait_ge(TC.h, wd_pn_free[sidx])
                scalar.activation(pnh[sidx], sb, ACT.Sign,
                                  scale=wspos[:, 2:3], bias=bm05[:, 0:1])
                smark = SC.inc(scalar.activation(sb, sb, ACT.Sign,
                                                 scale=wspos[:, 2:3], bias=bp05[:, 0:1]))
                gpsimd.wait_ge(SC.h, smark)
                gpsimd.wait_ge(TC.h, wd_pn_free[sidx])
                wd_slab_free[sidx] = GC.inc(
                    gpsimd.tensor_tensor(pnh[sidx], pnh[sidx], sb, ALU.add))
                tensor.wait_ge(GC.h, wd_slab_free[sidx])
                marks = []
                for k8 in range(2):
                    bank = 4 + (i % 2) * 2 + k8
                    if bank in wd_evmk:
                        tensor.wait_ge(SC.h, wd_evmk[bank])
                    else:
                        if bank in evac_mark:
                            tensor.wait_ge(VC.h, evac_mark[bank])
                        if bank in h128ev:
                            tensor.wait_ge(VC.h, h128ev[bank])
                    for kk in range(4):
                        k = k8 * 4 + kk
                        mm = tensor.transpose(pp[bank][:, kk * 128:(kk + 1) * 128],
                                              pnh[sidx][:, k * 128:(k + 1) * 128],
                                              identf)
                    marks.append(TC.inc(mm))
                wd_pn_free[sidx] = marks[-1]
                return marks

            def wd_evac(i, marks):
                r, og = wd_jobs[i]
                sidx = i % 2
                stg = 4096 + sidx * 1024
                scalar.wait_ge(GW.h, wd_stage_free[sidx])
                for k8 in range(2):
                    bank = 4 + (i % 2) * 2 + k8
                    scalar.wait_ge(TC.h, marks[k8])
                    wd_evmk[bank] = SC.inc(scalar.copy(
                        wtb[:, stg + k8 * 512: stg + (k8 + 1) * 512], pp[bank]))
                gpsimd.wait_ge(SC.h, wd_evmk[bank])
                wd_stage_free[sidx] = gsts(
                    GW, ternD_w[og][:, :, r * 128:(r + 1) * 128],
                    wtb[:, stg:stg + 1024].rearrange("p (t m) -> p t m", m=128))

            wd_pend = None
            for i in range(128):
                marks = wd_tern(i)
                if wd_pend is not None:
                    wd_evac(*wd_pend)
                wd_pend = (i, marks)
            wd_evac(*wd_pend)
        if debug and hf == 0:
            gpsimd.wait_ge(VC.h, VC.n)
            for a in range(NA):
                gst(dbg["dbg_int"][:, a * 128:(a + 1) * 128],
                    inter[:, a * HALF: a * HALF + 128])
            vector.wait_ge(G.h, G.n)

        for s in [1, 2, 4, 8, 16, 32]:
            if s == 1:
                for a in range(NA):
                    if a & s:
                        continue
                    b2 = a + s
                    Aap = inter[:, a * HALF:(a + 1) * HALF]
                    Bap = inter[:, b2 * HALF:(b2 + 1) * HALF]
                    vector.tensor_tensor(Aap, Aap, Bap, ALU.add)
                    vector.scalar_tensor_tensor(Bap, Bap, -2.0, Aap, ALU.mult, ALU.add)
            else:
                for a in range(0, NA, 2):
                    if a & s:
                        continue
                    b2 = a + s
                    Aap = inter[:, a * HALF:(a + 2) * HALF]
                    Bap = inter[:, b2 * HALF:(b2 + 2) * HALF]
                    vector.tensor_tensor(Aap, Aap, Bap, ALU.add)
                    vector.scalar_tensor_tensor(Bap, Bap, -2.0, Aap, ALU.mult, ALU.add)
        mminb = dqb[:, 1024:1536]
        vector.tensor_copy(mmaxb, inter[:, 0:HALF])
        vector.tensor_copy(mminb, inter[:, 0:HALF])
        for a in range(1, NA):
            vector.tensor_tensor(mmaxb, mmaxb, inter[:, a * HALF:(a + 1) * HALF], ALU.max)
            vector.tensor_tensor(mminb, mminb, inter[:, a * HALF:(a + 1) * HALF], ALU.min)
        VC.inc(vector.engine_nop())
        tensor.wait_ge(VC.h, VC.n)
        tensor.wait_ge(SC.h, SC.n)
        for j in range(NHT):
            mm = tensor.transpose(pp[6][:, j * 128:(j + 1) * 128],
                                  mmaxb[:, j * 128:(j + 1) * 128], identf)
            mm = tensor.transpose(pp[7][:, j * 128:(j + 1) * 128],
                                  mminb[:, j * 128:(j + 1) * 128], identf)
        TC.inc(mm)
        vector.wait_ge(TC.h, TC.n)
        vector.tensor_reduce(sm[:, 8:12], pp[6].rearrange("p (j n) -> p j n", n=128),
                             AX.X, ALU.max)
        VC.inc(vector.tensor_reduce(sm[:, 4:8], pp[7].rearrange("p (j n) -> p j n", n=128),
                                    AX.X, ALU.min))
        vector.wait_ge(VC.h, VC.n)
        vector.tensor_scalar(sm[:, 4:8], sm[:, 4:8], -1.0, None, ALU.mult)
        VC.inc(vector.engine_nop())
        vector.wait_ge(VC.h, VC.n)
        VC.inc(vector.tensor_tensor(sm[:, 8:12], sm[:, 8:12], sm[:, 4:8], ALU.max))
        gpsimd.wait_ge(VC.h, VC.n)
        GC.inc(gpsimd.tensor_scalar(amax2, sm[:, 8:12], 1e-5, None, ALU.max))
        vector.wait_ge(GC.h, GC.n)
        VC.inc(vector.reciprocal(sm[:, 8:12], amax2))
        scalar.wait_ge(VC.h, VC.n)
        SC.inc(scalar.copy(sm[:, 12:16], sm[:, 8:12]))
        vector.wait_ge(SC.h, SC.n)
        VC.inc(vector.tensor_scalar(sm[:, 12:16], sm[:, 12:16], 128.0, None, ALU.mult))
        vector.tensor_scalar(deq2, amax2, wmb[:, 2:3], 1.0 / 256.0, ALU.mult, ALU.mult)
        VC.inc(vector.engine_nop())
        gpsimd.wait_ge(VC.h, VC.n)
        g1 = GA.inc(gpsimd.dma_start(out=s2row_d.rearrange("o (c p) -> o p c", p=128)[0],
                                     in_=sm[:, 12:16], allow_slow_non_contiguous=True), 16)
        sync.wait_ge(GA.h, g1)
        lds(SDQ, s2bb, s2row_d.broadcast_to((128, HALF)))
        lv = lds(SDQ, cbuf[:, 2560:3072], s2row_d.broadcast_to((128, HALF)))
        vector.wait_ge(SDQ.h, lv)
        VC.inc(vector.engine_nop())
        s2bbW = cbuf[:, 2048:3072]
        d2w = pnb[:, 0:1024]
        mk2w = pnb[:, 1024:2048]

        def quantize_pair(a):
            pair = inter[:, a * HALF:(a + 2) * HALF]
            vector.tensor_tensor(pair, pair, s2bbW, ALU.mult)
            vector.tensor_scalar(pair, pair, 128.5, 255.4999, ALU.add, ALU.min)
            vector.tensor_copy(d2w, pair)
            for bit in SEARCH_BITS:
                vector.tensor_scalar(mk2w, d2w, bit, -bit, ALU.is_ge, ALU.mult)
                vector.tensor_tensor(d2w, d2w, mk2w, ALU.add)
            return VC.inc(vector.scalar_tensor_tensor(pair, pair, 128.0, d2w,
                                                      ALU.subtract, ALU.subtract))


        xq2b = wtb[:, 2048:2048 + 2 * 512]
        wds = wtb[:, 0:2 * 1024]
        barrier()
        cmb_marks = {}
        out_marks = {}
        cmb_ji = 0
        for p2 in range(2):
            m0 = p2 * 1024
            dn_mm = {0: TC.n, 1: TC.n}
            for a in range(NA):
                ab2 = a % 2
                if a >= 2:
                    sync.wait_ge(TC.h, dn_mm[ab2])
                lv = lds(SWD[ab2], wds[:, ab2 * 1024:(ab2 + 1) * 1024].rearrange(
                    "p (h n) -> p h n", h=1)[:, 0, :],
                    ternD_d[a * 128:(a + 1) * 128, m0:m0 + 1024])
                if p2 == 0 and a % 2 == 0:
                    quantize_pair(a)
                if a >= 2:
                    vector.wait_ge(TC.h, dn_mm[ab2])
                VC.inc(vector.tensor_copy(xq2b[:, ab2 * 512:(ab2 + 1) * 512],
                                          inter[:, a * HALF:(a + 1) * HALF]))
                tensor.wait_ge(SWD[ab2].h, lv)
                tensor.wait_ge(VC.h, VC.n)
                for tg in range(NHT):
                    for m2 in range(2):
                        mm = tensor.matmul(
                            pp[tg * 2 + m2],
                            xq2b[:, ab2 * 512 + tg * 128: ab2 * 512 + (tg + 1) * 128],
                            wds[:, ab2 * 1024 + m2 * 512: ab2 * 1024 + (m2 + 1) * 512],
                            start=(a == 0), stop=(a == NA - 1))
                dn_mm[ab2] = TC.inc(mm)
            scalar.wait_ge(TC.h, TC.n)
            sync.wait_ge(TC.h, TC.n)
            csetA = (gt1, gt2, gt3)
            csetB = (pnb[:, 0:512], cbuf[:, 2048:2560], cbuf[:, 2560:3072])
            csets = [csetA, csetB]
            cmb_jobs = [(tg, m2) for tg in range(NHT) for m2 in range(2)]
            for (tg, m2) in cmb_jobs:
                ji = cmb_ji
                cmb_ji += 1
                par = ji % 2
                cg1, cg2, cg3 = csets[par]
                mc = m0 + m2 * 512
                rowa = t0 + tg * 128
                if ji >= 2:
                    sync.wait_ge(VC.h, cmb_marks[ji - 2])
                lds(SCMB[par], cg2, x_d[rowa:rowa + 128, mc:mc + 512])
                ls = lds(SCMB[par], cg3, s_sp[rowa:rowa + 128, mc:mc + 512])
                if ji >= 2:
                    scalar.wait_ge(GOUT.h, out_marks[ji - 2])
                SC.inc(scalar.activation(cg1, pp[tg * 2 + m2], ACT.Silu,
                                         scale=deq2[:, tg:tg + 1]))
                vector.wait_ge(SC.h, SC.n)
                vector.wait_ge(SCMB[par].h, ls)
                vector.tensor_tensor(cg1, cg1, cg3, ALU.mult)
                cmb_marks[ji] = VC.inc(vector.tensor_tensor(cg1, cg1, cg2, ALU.add))
                gpsimd.wait_ge(VC.h, VC.n)
                out_marks[ji] = gsts(GOUT, out_d[rowa:rowa + 128, mc:mc + 512], cg1)

    if debug:
        barrier()
        for i in range(NT):
            for mb2 in range(4):
                lv = ld(gt2, s_sp[i * 128:(i + 1) * 128, mb2 * 512:(mb2 + 1) * 512])
                gpsimd.wait_ge(LD.h, lv)
                gm = gst(dbg["dbg_s2"][i * 128:(i + 1) * 128, mb2 * 512:(mb2 + 1) * 512], gt2)
                sync.wait_ge(G.h, gm)
    barrier()
    dq_t.__exit__(None, None, None)
    wt_t.__exit__(None, None, None)
    cbuf_t.__exit__(None, None, None)
    inter_t.__exit__(None, None, None)
    return nc, dbg


def _host_constants():
    identf = np.eye(128, dtype=np.float32)
    identb = np.eye(128, dtype=np.float32).astype(ml_dtypes.bfloat16)
    h = np.array([[1.0]], dtype=np.float64)
    while h.shape[0] < 128:
        h = np.block([[h, h], [h, -h]])
    c = np.float32(1.0 / np.sqrt(np.float64(8192.0)))
    h128 = (h.astype(np.float32) * c).astype(np.float32)
    m1 = np.full((128, 128), -1e9, dtype=np.float32)
    for t8 in range(8):
        for i in range(16):
            for j in range(i + 1):
                m1[t8 * 16 + i, t8 * 16 + j] = 0.0
    mask4 = np.concatenate([m1] * 4, axis=1)
    ones = np.ones((128, 128), dtype=np.float32)
    return identf, identb, h128, mask4, ones


def make_in_maps(inputs):
    x = np.ascontiguousarray(np.asarray(inputs["L0_attn_x"], dtype=np.float32).reshape(TOK, D))
    wqkv = np.ascontiguousarray(np.asarray(inputs["L0_attn_Wqkv"], dtype=np.float32).reshape(D, 6144))
    bq = np.ascontiguousarray(
        np.asarray(inputs["L0_attn_bqkv"], dtype=np.float32).reshape(6144).reshape(48, 128).T)
    wout = np.ascontiguousarray(np.asarray(inputs["L0_attn_Wout"], dtype=np.float32).reshape(D, D))
    bout = np.ascontiguousarray(np.asarray(inputs["L0_attn_bout"], dtype=np.float32).reshape(1, D))
    wg = np.ascontiguousarray(np.asarray(inputs["L0_ffn_w_gate"], dtype=np.float32))
    wu = np.ascontiguousarray(np.asarray(inputs["L0_ffn_w_up"], dtype=np.float32))
    wd = np.ascontiguousarray(np.asarray(inputs["L0_ffn_w_down"], dtype=np.float32))
    identf, identb, h128, mask4, ones = _host_constants()
    maps = []
    for c in range(NCORES):
        maps.append({
            "x": x[c * TPC:(c + 1) * TPC], "wqkv": wqkv, "bqkv": bq, "wout": wout,
            "bout": bout, "wg": wg, "wu": wu, "wd": wd, "identf": identf,
            "h128": h128, "mask4": mask4, "ones": ones,
        })
    return maps


def kernel(L0_attn_x, L0_attn_Wqkv, L0_attn_bqkv, L0_attn_Wout, L0_attn_bout,
           L0_ffn_w_gate, L0_ffn_w_up, L0_ffn_w_down):
    maps = make_in_maps(dict(
        L0_attn_x=L0_attn_x, L0_attn_Wqkv=L0_attn_Wqkv, L0_attn_bqkv=L0_attn_bqkv,
        L0_attn_Wout=L0_attn_Wout, L0_attn_bout=L0_attn_bout,
        L0_ffn_w_gate=L0_ffn_w_gate, L0_ffn_w_up=L0_ffn_w_up,
        L0_ffn_w_down=L0_ffn_w_down))
    nc, _ = build_nc(debug=False, upto="C")
    res = run_bass_kernel_spmd(nc, maps, core_ids=list(range(NCORES)))
    out = np.concatenate([np.asarray(r["out"]) for r in res.results], axis=0)
    return out.reshape(4, 2048, D)


# version 12
# speedup vs baseline: 1.0655x; 1.0655x over previous
import numpy as np
import ml_dtypes
import concourse.bass as bass
import concourse.mybir as mybir
from concourse.bass_utils import run_bass_kernel_spmd

F32 = mybir.dt.float32
BF16 = mybir.dt.bfloat16
ALU = mybir.AluOpType
ACT = mybir.ActivationFunctionType
AX = mybir.AxisListType

NCORES = 8
TOK = 8192
TPC = TOK // NCORES
D = 2048
HALF = 512
NT = TPC // 128
NHT = HALF // 128
IFF = 8192
NA = IFF // 128
INV_SQRT_D3 = float(np.float32(1.0 / np.sqrt(np.float32(128.0))))
SEARCH_BITS = [128.0, 64.0, 32.0, 16.0, 8.0, 4.0, 2.0, 1.0]


class Sem:
    __slots__ = ("h", "n")

    def __init__(self, nc, name):
        self.h = nc.alloc_semaphore(name)
        self.n = 0

    def inc(self, instr, k=1):
        instr.then_inc(self.h, k)
        self.n += k
        return self.n


def build_nc(debug=False, upto="C"):
    nc = bass.Bass("TRN2", target_bir_lowering=False)

    x_d = nc.dram_tensor("x", [TPC, D], F32, kind="ExternalInput").ap()
    wqkv_d = nc.dram_tensor("wqkv", [D, 6144], F32, kind="ExternalInput").ap()
    bqkv_d = nc.dram_tensor("bqkv", [128, 48], F32, kind="ExternalInput").ap()
    wout_d = nc.dram_tensor("wout", [D, D], F32, kind="ExternalInput").ap()
    bout_d = nc.dram_tensor("bout", [1, D], F32, kind="ExternalInput").ap()
    wg_d = nc.dram_tensor("wg", [IFF, D], F32, kind="ExternalInput").ap()
    wu_d = nc.dram_tensor("wu", [IFF, D], F32, kind="ExternalInput").ap()
    wd_d = nc.dram_tensor("wd", [D, IFF], F32, kind="ExternalInput").ap()
    identf_d = nc.dram_tensor("identf", [128, 128], F32, kind="ExternalInput").ap()
    h128_d = nc.dram_tensor("h128", [128, 128], F32, kind="ExternalInput").ap()
    mask4_d = nc.dram_tensor("mask4", [128, 512], F32, kind="ExternalInput").ap()
    ones_d = nc.dram_tensor("ones", [128, 128], F32, kind="ExternalInput").ap()
    out_d = nc.dram_tensor("out", [TPC, D], F32, kind="ExternalOutput").ap()
    dbg = {}
    if debug:
        for nm, shp, dt in [("dbg_xn", [TPC, D], F32), ("dbg_xq", [D, TPC], BF16),
                            ("dbg_rs", [128, 64], F32), ("dbg_sv", [128, 512], F32),
                            ("dbg_s", [TPC, D], F32), ("dbg_means", [128, 4], F32),
                            ("dbg_int", [128, IFF], F32), ("dbg_xq2", [128, IFF], F32),
                            ("dbg_sm", [128, 128], F32), ("dbg_s2", [TPC, D], F32)]:
            dbg[nm] = nc.dram_tensor(nm, shp, dt, kind="ExternalOutput").ap()

    s_sp = nc.dram_tensor("s_sp", [TPC, D], F32, kind="Internal").ap()
    rs_dram = nc.dram_tensor("rs_dram", [512, 16], F32, kind="Internal").ap()
    a1row_d = nc.dram_tensor("a1row", [1, TPC], F32, kind="Internal").ap()
    s2row_d = nc.dram_tensor("s2row", [1, HALF], F32, kind="Internal").ap()
    ternG_d = nc.dram_tensor("ternG", [NA * 16 * 128, 128], BF16, kind="Internal").ap()
    ternU_d = nc.dram_tensor("ternU", [NA * 16 * 128, 128], BF16, kind="Internal").ap()
    ternD_d = nc.dram_tensor("ternD", [IFF, D], BF16,
                             kind="ExternalOutput" if debug else "Internal").ap()

    LD = Sem(nc, "LD")
    G = Sem(nc, "G")
    Q = Sem(nc, "Q")
    TC = Sem(nc, "TC")
    VC = Sem(nc, "VC")
    SC = Sem(nc, "SC")
    GC = Sem(nc, "GC")
    SX = [Sem(nc, "SX0"), Sem(nc, "SX1")]
    SW = [Sem(nc, "SW0"), Sem(nc, "SW1"), Sem(nc, "SW2")]
    SBB = Sem(nc, "SBB")
    SSL = [Sem(nc, "SSL0"), Sem(nc, "SSL1")]
    SWT = [Sem(nc, "SWTa"), Sem(nc, "SWTb")]
    SWD = [Sem(nc, "SWD0"), Sem(nc, "SWD1")]
    SDQ = Sem(nc, "SDQ")
    SCMB = [Sem(nc, "SCMB0"), Sem(nc, "SCMB1")]
    QM = [Sem(nc, "QM%d" % i) for i in range(8)]
    GS = [Sem(nc, "GS0"), Sem(nc, "GS1")]
    GT = Sem(nc, "GT")
    GW = Sem(nc, "GW")
    GR = Sem(nc, "GR")
    GA = Sem(nc, "GA")
    GOUT = Sem(nc, "GOUT")
    ALL_SEMS = [LD, G, Q, TC, VC, SC, GC, SBB, SDQ, GT, GW, GR, GA, GOUT] + SX + SW + SSL + SWD + QM + GS + SWT + SCMB

    A = lambda name, shape, dt: nc.alloc_sbuf_tensor("sb_" + name, shape, dt).ap()
    identf = A("identf", [128, 128], F32)
    h128 = A("h128", [128, 128], F32)
    ones = A("ones", [128, 128], F32)
    amax1 = A("amax1", [128, NT], F32)
    sm = A("sm", [128, 16], F32)
    msums = A("msums", [128, 4], F32)
    msw = A("msw", [128, 768], F32)
    wmean = A("wmean", [128, 4], F32)
    wspos = A("wspos", [128, 4], F32)
    wsposb = A("wsposb", [128, 4], F32)
    wsneg = A("wsneg", [128, 4], F32)
    wmb = A("wmb", [128, 4], F32)
    m4 = A("m4", [128, 4], F32)
    m4n = A("m4n", [128, 4], F32)
    s4 = A("s4", [128, 4], F32)
    amax2 = A("amax2", [128, 4], F32)
    deq2 = A("deq2", [128, 4], F32)
    bm05 = A("bm05", [128, 1], F32)
    bp05 = A("bp05", [128, 1], F32)
    xqT = A("xqT", [128, 16 * TPC], BF16)

    pp = [nc.alloc_psum_tensor("pp%d" % i, [128, 512], F32).ap() for i in range(8)]

    sync, tensor, vector, scalar, gpsimd = nc.sync, nc.tensor, nc.vector, nc.scalar, nc.gpsimd

    def ld(dst, src):
        LD.inc(sync.dma_start(out=dst, in_=src), 16)
        return LD.n

    def gst(dst, src):
        G.inc(gpsimd.dma_start(out=dst, in_=src), 16)
        return G.n

    def lds(sem, dst, src2, **kw):
        sem.inc(sync.dma_start(out=dst, in_=src2, **kw), 16)
        return sem.n

    def gsts(sem, dst, src2, **kw):
        sem.inc(gpsimd.dma_start(out=dst, in_=src2, **kw), 16)
        return sem.n

    def barrier():
        VC.inc(vector.engine_nop())
        SC.inc(scalar.copy(sm[:, 15:16], sm[:, 15:16]))
        for eng in (sync, tensor, vector, scalar, gpsimd):
            for s in ALL_SEMS:
                if s.n:
                    eng.wait_ge(s.h, s.n)

    ld(identf, identf_d)
    ld(h128, h128_d)
    ld(ones, ones_d)
    vector.wait_ge(LD.h, LD.n)
    vector.memset(sm[:, 5:6], 1e-5)
    vector.memset(bm05, -0.5)
    vector.memset(bp05, 0.5)
    VC.inc(vector.memset(msums, 0.0))

    mch_t = nc.sbuf_tensor("mchunk", [128, 3072], F32)
    mchunk = mch_t.__enter__().ap()
    means_jobs = []
    for mi, wsrc in enumerate([wg_d, wu_d, wd_d]):
        flat = wsrc.rearrange("a b -> (a b)").rearrange("(p n) -> p n", p=128)
        for j in range(256):
            means_jobs.append((mi, flat[:, j * 512:(j + 1) * 512]))
    mst = {"issued": 0, "done": 0}
    means_abs = {}
    means_marks = {}

    def means_issue():
        i = mst["issued"]
        if i >= len(means_jobs):
            return
        _, src = means_jobs[i]
        buf = mchunk[:, (i % 6) * 512:((i % 6) + 1) * 512]
        if i >= 6:
            scalar.wait_ge(SC.h, means_abs[i - 6])
        means_marks[i] = QM[i % 6].inc(scalar.dma_start(out=buf, in_=src), 16)
        mst["issued"] = i + 1

    def means_step(k=1):
        for _ in range(k):
            i = mst["done"]
            if i >= len(means_jobs):
                return
            while mst["issued"] < min(i + 6, len(means_jobs)):
                means_issue()
            mi, _ = means_jobs[i]
            buf = mchunk[:, (i % 6) * 512:((i % 6) + 1) * 512]
            scalar.wait_ge(QM[i % 6].h, means_marks[i])
            means_abs[i] = SC.inc(
                scalar.activation(buf, buf, ACT.Abs, accum_out=msw[:, i:i + 1]))
            mst["done"] = i + 1

    xnT_t = nc.sbuf_tensor("xnT", [128, 16 * TPC], F32)
    xnT = xnT_t.__enter__().ap()
    workA_t = nc.sbuf_tensor("workA", [128, 4 * D], F32)
    workA = workA_t.__enter__().ap()

    xb = [workA[:, 0:D], workA[:, D:2 * D]]
    xnb = workA[:, 2 * D:3 * D]
    mkb = workA[:, 3 * D:4 * D]

    vc_tile = [0, 0]
    evac_mark = {}
    for i in range(NT):
        b = i % 2
        if vc_tile[b]:
            sync.wait_ge(VC.h, vc_tile[b])
        lv = lds(SX[b], xb[b], x_d[i * 128:(i + 1) * 128, :])
        vector.wait_ge(SX[b].h, lv)
        VC.inc(vector.tensor_reduce(sm[:, 0:1], xb[b], AX.X, ALU.add))
        gpsimd.wait_ge(VC.h, VC.n)
        GC.inc(gpsimd.tensor_scalar(sm[:, 1:2], sm[:, 0:1], -1.0 / 2048.0, None, ALU.mult))
        scalar.wait_ge(GC.h, GC.n)
        SC.inc(scalar.activation(xnb, xb[b], ACT.Identity, bias=sm[:, 1:2], scale=1.0))
        vector.wait_ge(SC.h, SC.n)
        vector.tensor_tensor(mkb, xnb, xnb, ALU.mult)
        VC.inc(vector.tensor_reduce(sm[:, 2:3], mkb, AX.X, ALU.add))
        scalar.wait_ge(VC.h, VC.n)
        SC.inc(scalar.activation(sm[:, 3:4], sm[:, 2:3], ACT.Sqrt, bias=sm[:, 5:6],
                                 scale=1.0 / 2048.0))
        vector.wait_ge(SC.h, SC.n)
        VC.inc(vector.reciprocal(sm[:, 4:5], sm[:, 3:4]))
        gpsimd.wait_ge(VC.h, VC.n)
        GC.inc(gpsimd.tensor_copy(sm[:, 12:13], sm[:, 4:5]))
        vector.wait_ge(GC.h, GC.n)
        vector.tensor_scalar(xnb, xnb, sm[:, 12:13], None, ALU.mult)
        VC.inc(vector.tensor_reduce(sm[:, 13:14], xnb, AX.X, ALU.max,
                                    apply_absolute_value=True))
        gpsimd.wait_ge(VC.h, VC.n)
        GC.inc(gpsimd.tensor_scalar(amax1[:, i:i + 1], sm[:, 13:14], 1e-5, None, ALU.max))
        vector.wait_ge(GC.h, GC.n)
        VC.inc(vector.reciprocal(sm[:, 6:7], amax1[:, i:i + 1]))
        scalar.wait_ge(VC.h, VC.n)
        SC.inc(scalar.copy(sm[:, 7:8], sm[:, 6:7]))
        if debug:
            gpsimd.wait_ge(SC.h, SC.n)
            gdm = gst(dbg["dbg_sm"][:, i * 16:(i + 1) * 16], sm)
            vector.wait_ge(G.h, gdm)
        tensor.wait_ge(VC.h, VC.n)
        for k4 in range(4):
            bank = k4 % 2
            if bank in evac_mark:
                tensor.wait_ge(VC.h, evac_mark[bank])
            for kk in range(4):
                k = k4 * 4 + kk
                mm = tensor.transpose(pp[bank][:, kk * 128:(kk + 1) * 128],
                                      xnb[:, k * 128:(k + 1) * 128], identf)
            TC.inc(mm)
            vector.wait_ge(TC.h, TC.n)
            dst = xnT.rearrange("p (k t) -> p k t", t=TPC)[:, k4 * 4:(k4 + 1) * 4,
                                                          i * 128:(i + 1) * 128]
            evac_mark[bank] = VC.inc(
                vector.tensor_copy(dst, pp[bank].rearrange("p (k t) -> p k t", t=128)))
        if debug:
            gpsimd.wait_ge(VC.h, VC.n)
            gm = gst(dbg["dbg_xn"][i * 128:(i + 1) * 128, :], xnb)
            vector.wait_ge(G.h, gm)
        u, dd = xb[b], xnb
        vector.wait_ge(SC.h, SC.n)
        vector.tensor_scalar(u, dd, sm[:, 7:8], 128.0, ALU.mult, ALU.mult)
        vector.tensor_scalar(u, u, 128.5, 255.4999, ALU.add, ALU.min)
        vector.tensor_copy(dd, u)
        for bit in SEARCH_BITS:
            vector.tensor_scalar(mkb, dd, bit, -bit, ALU.is_ge, ALU.mult)
            vector.tensor_tensor(dd, dd, mkb, ALU.add)
        VC.inc(vector.scalar_tensor_tensor(mkb, u, 128.0, dd, ALU.subtract, ALU.subtract))
        vc_tile[b] = VC.n
        tensor.wait_ge(VC.h, VC.n)
        for k4 in range(4):
            bank = 2 + (k4 % 2)
            if bank in evac_mark:
                tensor.wait_ge(VC.h, evac_mark[bank])
            for kk in range(4):
                k = k4 * 4 + kk
                mm = tensor.transpose(pp[bank][:, kk * 128:(kk + 1) * 128],
                                      mkb[:, k * 128:(k + 1) * 128], identf)
            TC.inc(mm)
            vector.wait_ge(TC.h, TC.n)
            dstq = xqT.rearrange("p (k t) -> p k t", t=TPC)[:, k4 * 4:(k4 + 1) * 4,
                                                           i * 128:(i + 1) * 128]
            evac_mark[bank] = VC.inc(
                vector.tensor_copy(dstq, pp[bank].rearrange("p (k t) -> p k t", t=128)))
        means_step(8)

    if debug:
        gpsimd.wait_ge(VC.h, VC.n)
        for k in range(16):
            gst(dbg["dbg_xq"][k * 128:(k + 1) * 128, :], xqT[:, k * TPC:(k + 1) * TPC])

    gpsimd.wait_ge(GC.h, GC.n)
    GA.inc(gpsimd.dma_start(out=a1row_d.rearrange("o (c p) -> o p c", p=128)[0],
                            in_=amax1, allow_slow_non_contiguous=True), 16)

    if upto == "A":
        barrier()
        return nc, dbg

    workA_t.__exit__(None, None, None)
    wqs_t = nc.sbuf_tensor("wqs", [128, 3 * 1024], F32)
    wqs = wqs_t.__enter__().ap()
    qkT_t = nc.sbuf_tensor("qkT", [128, 32 * HALF], F32)
    qkT = qkT_t.__enter__().ap()
    workB_t = nc.sbuf_tensor("workB", [128, 1024], F32)
    workB = workB_t.__enter__().ap()
    bsm_t = nc.sbuf_tensor("bsm", [128, 2048 + 1712], F32)
    bsm = bsm_t.__enter__().ap()
    rs_all = bsm[:, 0:64]
    rs_t = bsm[:, 64:128]
    sall = bsm[:, 1536 + 256:1536 + 320]
    sallb = bsm[:, 1536 + 320:1536 + 384]
    rall = bsm[:, 1536 + 384:1536 + 448]
    rallb = bsm[:, 1536 + 448:1536 + 512]
    mask4 = bsm[:, 2048:2560]
    scb = bsm[:, 2560:3584]
    t1sb = bsm[:, 3584:3712]
    bqkv = bsm[:, 3712:3760]
    ld(mask4, mask4_d)
    ld(bqkv, bqkv_d)
    svT = bsm[:, 128:128 + HALF]
    sv_t = bsm[:, 128 + HALF:128 + 2 * HALF]
    bb = bsm[:, 128 + 2 * HALF:128 + 2 * HALF + 512]
    attn_s = qkT[:, 8192:8192 + D]
    sst = [workB[:, 0:512], workB[:, 512:1024]]


    wq_r = wqkv_d.rearrange("(k p) n -> p k n", p=128)
    barrier()

    for hf in range(2):
        t0 = hf * HALF
        tensor.wait_ge(VC.h, VC.n)
        tensor.wait_ge(SC.h, SC.n)
        sync.wait_ge(TC.h, TC.n)
        VC.inc(vector.memset(svT, 0.0))
        evac_q = {0: SC.n, 1: SC.n}
        evac_v = {0: VC.n, 1: VC.n}
        mm_done = {0: TC.n, 1: TC.n}
        slot_mm = {0: TC.n, 1: TC.n, 2: TC.n}
        for c in range(48):
            b = c % 2
            if c >= 2:
                tensor.wait_ge(SC.h, evac_q[b])
                tensor.wait_ge(VC.h, evac_v[b])
            for hs in range(2):
                s4i = (2 * c + hs) % 3
                sync.wait_ge(TC.h, slot_mm[s4i])
                dstw = wqs[:, s4i * 1024:(s4i + 1) * 1024].rearrange("p (k c) -> p k c", c=128)
                lv = lds(SW[s4i], dstw, wq_r[:, 8 * hs:8 * hs + 8, c * 128:(c + 1) * 128])
                tensor.wait_ge(SW[s4i].h, lv)
                for kk in range(8):
                    k = 8 * hs + kk
                    mm = tensor.matmul(pp[b],
                                       wqs[:, s4i * 1024 + kk * 128: s4i * 1024 + (kk + 1) * 128],
                                       xnT[:, k * TPC + t0: k * TPC + t0 + HALF],
                                       start=(k == 0), stop=(k == 15))
                slot_mm[s4i] = TC.inc(mm)
            mm_done[b] = slot_mm[(2 * c + 1) % 3]
            h = c // 3
            part = c % 3
            if part < 2:
                dstqk = qkT.rearrange("p (r t h2) -> p r t h2", r=2, h2=16)[:, part, :, h]
                scalar.wait_ge(TC.h, TC.n)
                evac_q[b] = SC.inc(
                    scalar.activation(dstqk, pp[b],
                                      ACT.Identity, bias=bqkv[:, c:c + 1], scale=1.0))
            else:
                vector.wait_ge(TC.h, TC.n)
                evac_v[b] = VC.inc(
                    vector.scalar_tensor_tensor(svT, pp[b], bqkv[:, c:c + 1], svT,
                                                ALU.add, ALU.add))
            means_step(5)
        if upto == "B1":
            break
        scalar.wait_ge(VC.h, VC.n)
        sc_bank_free = {2: VC.n, 3: VC.n}
        for gb in range(16):
            bank = 2 + gb % 2
            if gb == 0:
                tensor.wait_ge(SC.h, SC.n)
            tensor.wait_ge(VC.h, sc_bank_free[bank])
            for q4 in range(4):
                g = gb * 4 + q4
                qap = qkT[:, g * 128:(g + 1) * 128]
                kap = qkT[:, 8192 + g * 128: 8192 + (g + 1) * 128]
                mm = tensor.matmul(pp[bank][:, q4 * 128:(q4 + 1) * 128], qap, kap,
                                   start=True, stop=True)
            TC.inc(mm)
            vector.wait_ge(TC.h, TC.n)
            vector.scalar_tensor_tensor(scb[:, 0:512], pp[bank], INV_SQRT_D3, mask4,
                                        ALU.mult, ALU.add)
            sc_bank_free[bank] = VC.inc(vector.engine_nop())
            VC.inc(vector.tensor_reduce(m4, scb[:, 0:512].rearrange("p (g n) -> p g n", n=128),
                                         AX.X, ALU.max))
            gpsimd.wait_ge(VC.h, VC.n)
            GC.inc(gpsimd.tensor_scalar(m4n, m4, -1.0, None, ALU.mult))
            scalar.wait_ge(GC.h, GC.n)
            for q4 in range(4):
                mm2 = scalar.activation(scb[:, 512 + q4 * 128: 512 + (q4 + 1) * 128],
                                        scb[:, q4 * 128:(q4 + 1) * 128], ACT.Exp,
                                        bias=m4n[:, q4:q4 + 1], scale=1.0)
            SC.inc(mm2)
            vector.wait_ge(SC.h, SC.n)
            VC.inc(vector.tensor_reduce(sall[:, gb * 4:(gb + 1) * 4],
                                        scb[:, 512:1024].rearrange("p (g n) -> p g n", n=128),
                                        AX.X, ALU.add))
            means_step(2)
        if upto == "B2":
            break
        gpsimd.wait_ge(VC.h, VC.n)
        GC.inc(gpsimd.tensor_copy(sallb, sall))
        vector.wait_ge(GC.h, GC.n)
        VC.inc(vector.reciprocal(rall, sallb))
        scalar.wait_ge(VC.h, VC.n)
        SC.inc(scalar.copy(rallb, rall))
        vector.wait_ge(SC.h, SC.n)
        VC.inc(vector.tensor_tensor(rs_all[:, 0:64], sallb, rallb, ALU.mult))
        tensor.wait_ge(VC.h, VC.n)
        TC.inc(tensor.transpose(pp[4][0:64, 0:128], rs_all[:, 0:64], identf))
        vector.wait_ge(TC.h, TC.n)
        VC.inc(vector.tensor_copy(t1sb[0:64, :], pp[4][0:64, 0:128]))
        gpsimd.wait_ge(VC.h, VC.n)
        g1 = gsts(GR, rs_dram.rearrange("(g r) i -> g (r i)", r=8), t1sb[0:64, 0:128])
        gpsimd.wait_ge(GR.h, g1)
        rs_sh = gsts(GR, rs_t[:, 0:64].rearrange("p (j i) -> p j i", i=16),
                     rs_dram.rearrange("(j p) i -> p j i", p=128))
        for j in range(NHT):
            mm = tensor.transpose(pp[5][:, j * 128:(j + 1) * 128],
                                  svT[:, j * 128:(j + 1) * 128], identf)
        TC.inc(mm)
        vector.wait_ge(TC.h, TC.n)
        VC.inc(vector.tensor_copy(sv_t, pp[5]))
        if debug and hf == 0:
            gpsimd.wait_ge(VC.h, VC.n)
            gpsimd.wait_ge(G.h, rs_sh)
            gst(dbg["dbg_rs"], rs_t)
            gst(dbg["dbg_sv"], sv_t)
        if upto == "B3":
            break
        attnT = qkT[:, 0:16 * HALF]
        vector.wait_ge(GR.h, rs_sh)
        for j in range(NHT):
            for i in range(16):
                vector.tensor_scalar(attn_s[:, i * 128:(i + 1) * 128],
                                     sv_t[:, j * 128:(j + 1) * 128],
                                     rs_t[:, j * 16 + i:j * 16 + i + 1], None, ALU.mult)
            VC.inc(vector.engine_nop())
            tensor.wait_ge(VC.h, VC.n)
            for c4 in range(4):
                bank = 6 + (c4 % 2)
                if bank in evac_mark:
                    tensor.wait_ge(VC.h, evac_mark[bank])
                for cc in range(4):
                    c = c4 * 4 + cc
                    mm = tensor.transpose(pp[bank][:, cc * 128:(cc + 1) * 128],
                                          attn_s[:, c * 128:(c + 1) * 128], identf)
                TC.inc(mm)
                vector.wait_ge(TC.h, TC.n)
                dsta = attnT.rearrange("p (c t) -> p c t", t=HALF)[:, c4 * 4:(c4 + 1) * 4,
                                                                  j * 128:(j + 1) * 128]
                evac_mark[bank] = VC.inc(
                    vector.tensor_copy(dsta, pp[bank].rearrange("p (c t) -> p c t", t=128)))
            means_step(6)
        if upto == "B5":
            break
        s_done = [0, 0]
        bb_free = 0
        wout_mm = {0: TC.n, 1: TC.n}
        evac_vc = VC.n
        for mb in range(4):
            if bb_free:
                sync.wait_ge(VC.h, bb_free)
            bb_done = lds(SBB, bb, bout_d[:, mb * 512:(mb + 1) * 512].broadcast_to((128, 512)))
            tensor.wait_ge(VC.h, evac_vc)
            if mb == 0:
                tensor.wait_ge(SC.h, SC.n)
            for hd in range(16):
                b = hd % 2
                sync.wait_ge(TC.h, wout_mm[b])
                lv = lds(SW[b], wqs[:, b * 1024: b * 1024 + 512],
                         wout_d[hd * 128:(hd + 1) * 128, mb * 512:(mb + 1) * 512])
                tensor.wait_ge(SW[b].h, lv)
                for j in range(NHT):
                    mm = tensor.matmul(pp[j],
                                       attnT[:, hd * HALF + j * 128: hd * HALF + (j + 1) * 128],
                                       wqs[:, b * 1024: b * 1024 + 512],
                                       start=(hd == 0), stop=(hd == 15))
                wout_mm[b] = TC.inc(mm)
            vector.wait_ge(TC.h, TC.n)
            vector.wait_ge(SBB.h, bb_done)
            for j in range(NHT):
                bs = j % 2
                if s_done[bs]:
                    vector.wait_ge(GS[bs].h, s_done[bs])
                bb_free = VC.inc(vector.tensor_tensor(sst[bs], pp[j], bb, ALU.add))
                gpsimd.wait_ge(VC.h, VC.n)
                s_done[bs] = gsts(
                    GS[bs],
                    s_sp[t0 + j * 128: t0 + (j + 1) * 128, mb * 512:(mb + 1) * 512],
                    sst[bs])
                if debug:
                    s_done[bs] = gsts(
                        GS[bs],
                        dbg["dbg_s"][t0 + j * 128: t0 + (j + 1) * 128,
                                     mb * 512:(mb + 1) * 512], sst[bs])
            evac_vc = VC.n
            means_step(8)
    means_step(len(means_jobs))

    barrier()
    bsm_t.__exit__(None, None, None)
    workB_t.__exit__(None, None, None)
    qkT_t.__exit__(None, None, None)
    wqs_t.__exit__(None, None, None)
    xnT_t.__exit__(None, None, None)
    mch_t.__exit__(None, None, None)

    if upto.startswith("B"):
        return nc, dbg

    vector.wait_ge(SC.h, SC.n)
    vector.memset(msums, 0.0)
    for mi3 in range(3):
        vector.tensor_reduce(msums[:, mi3:mi3 + 1], msw[:, mi3 * 256:(mi3 + 1) * 256],
                             AX.X, ALU.add)
    VC.inc(vector.engine_nop())
    tensor.wait_ge(VC.h, VC.n)
    TC.inc(tensor.matmul(pp[4][:, 0:4], ones, msums, start=True, stop=True))
    vector.wait_ge(TC.h, TC.n)
    VC.inc(vector.tensor_scalar(wmean[:, 0:4], pp[4][:, 0:4], 1.0 / 16777216.0, 1e-5,
                                ALU.mult, ALU.max))
    gpsimd.wait_ge(VC.h, VC.n)
    GC.inc(gpsimd.tensor_copy(wmb[:, 0:4], wmean[:, 0:4]))
    vector.wait_ge(GC.h, GC.n)
    VC.inc(vector.reciprocal(wspos[:, 0:4], wmb[:, 0:4]))
    gpsimd.wait_ge(VC.h, VC.n)
    GC.inc(gpsimd.tensor_scalar(wsneg[:, 0:4], wspos[:, 0:4], -1.0, None, ALU.mult))
    scalar.wait_ge(VC.h, VC.n)
    SC.inc(scalar.copy(wsposb[:, 0:4], wspos[:, 0:4]))
    if debug:
        gpsimd.wait_ge(VC.h, VC.n)
        gst(dbg["dbg_means"], wmb)

    inter_t = nc.sbuf_tensor("inter", [128, NA * HALF], F32)
    inter = inter_t.__enter__().ap()
    cbuf_t = nc.sbuf_tensor("cbuf", [128, 2 * 1024 + 1024 + 2048], F32)
    cbuf = cbuf_t.__enter__().ap()
    wt_t = nc.sbuf_tensor("wt", [128, 8 * 1024], BF16)
    wtb = wt_t.__enter__().ap()
    dq_t = nc.sbuf_tensor("dq", [128, 3 * 512], F32)
    dqb = dq_t.__enter__().ap()

    slab = [cbuf[:, i * 1024:(i + 1) * 1024] for i in range(2)]
    s2bb = cbuf[:, 2048:2560]
    mmaxb = cbuf[:, 2560:3072]
    pnb = cbuf[:, 3072:5120]
    wtg2 = [wtb[:, 0:2048], wtb[:, 2048:4096]]
    wtu2 = [wtb[:, 4096:6144], wtb[:, 6144:8192]]
    deqg = dqb[:, 0:512]
    dequ = dqb[:, 512:1024]
    gt1 = dqb[:, 1024:1536]
    gt2 = deqg
    gt3 = dequ

    wg_r = wg_d.rearrange("(a p) h -> a p h", p=128)
    wu_r = wu_d.rearrange("(a p) h -> a p h", p=128)
    ternG_r = ternG_d.rearrange("(a k p) o -> a p k o", k=16, p=128)
    ternU_r = ternU_d.rearrange("(a k p) o -> a p k o", k=16, p=128)
    barrier()

    for hf in range(2):
        t0 = hf * HALF
        if hf == 1:
            for eng in (vector, scalar, gpsimd, tensor, sync):
                eng.wait_ge(GOUT.h, GOUT.n)
                eng.wait_ge(SCMB[0].h, SCMB[0].n)
                eng.wait_ge(SCMB[1].h, SCMB[1].n)
        lv = lds(SDQ, deqg, a1row_d[:, t0:t0 + HALF].broadcast_to((128, HALF)))
        lv = lds(SDQ, dequ, a1row_d[:, t0:t0 + HALF].broadcast_to((128, HALF)))
        vector.wait_ge(SDQ.h, lv)
        vector.tensor_scalar(deqg, deqg, wmb[:, 0:1], 1.0 / 256.0, ALU.mult, ALU.mult)
        VC.inc(vector.tensor_scalar(dequ, dequ, wmb[:, 1:2], 1.0 / 256.0,
                                    ALU.mult, ALU.mult))

        slab_free = {i: GC.n for i in range(2)}
        h128ev = {6: VC.n, 7: VC.n}
        pnh = [pnb[:, 0:1024], pnb[:, 1024:2048]]
        pn_free = {0: TC.n, 1: TC.n}
        hsc_state = {"c": 0}
        wt_free = {0: TC.n, 1: TC.n}
        bank_free = {0: VC.n, 1: VC.n, 2: VC.n, 3: VC.n}
        tern_spill = {0: G.n, 1: G.n}
        wt_ready = {}
        gu_mm = {}
        inter_done = {}
        sil_mark = {}
        c1silu = cbuf[:, 2048:2560]
        c1up = cbuf[:, 2560:3072]

        def c1_produce(a):
            ab = a % 2
            wtg = wtg2[ab]
            wtu = wtu2[ab]
            if hf == 0:
                groups = []
                for mi2, (wsrc, tdst, wtdst, scol) in enumerate(
                        [(wg_r, ternG_r, wtg, 0), (wu_r, ternU_r, wtu, 1)]):
                    for hs in range(2):
                        groups.append((mi2, wsrc, wtdst, scol, hs))

                def g_tern(gi):
                    mi2, wsrc, wtdst, scol, hs = groups[gi]
                    hc = hsc_state["c"] % 2
                    hsc_state["c"] += 1
                    sb = slab[hs]
                    sync.wait_ge(GC.h, slab_free[hs])
                    lv = lds(SSL[hs], sb, wsrc[a, :, hs * 1024:(hs + 1) * 1024])
                    scalar.wait_ge(SSL[hs].h, lv)
                    scalar.wait_ge(TC.h, pn_free[hc])
                    scalar.activation(pnh[hc], sb, ACT.Sign,
                                      scale=wspos[:, scol:scol + 1], bias=bm05[:, 0:1])
                    smark = SC.inc(scalar.activation(
                        sb, sb, ACT.Sign,
                        scale=wspos[:, scol:scol + 1], bias=bp05[:, 0:1]))
                    gpsimd.wait_ge(SC.h, smark)
                    slab_free[hs] = GC.inc(
                        gpsimd.tensor_tensor(pnh[hc], pnh[hc], sb, ALU.add))
                    tensor.wait_ge(GC.h, slab_free[hs])
                    marks = []
                    for k8 in range(2):
                        bank = 4 + (gi % 2) * 2 + k8
                        if bank in evac_mark:
                            tensor.wait_ge(VC.h, evac_mark[bank])
                        if bank in h128ev:
                            tensor.wait_ge(VC.h, h128ev[bank])
                        for kk in range(4):
                            k = k8 * 4 + kk
                            mm = tensor.transpose(pp[bank][:, kk * 128:(kk + 1) * 128],
                                                  pnh[hc][:, k * 128:(k + 1) * 128],
                                                  identf)
                        marks.append(TC.inc(mm))
                    pn_free[hc] = marks[-1]
                    return marks

                def g_evac(gi, marks):
                    mi2, wsrc, wtdst, scol, hs = groups[gi]
                    if a >= 2 and gi == 0:
                        vector.wait_ge(TC.h, wt_free[ab])
                        vector.wait_ge(GT.h, tern_spill[ab])
                    for k8 in range(2):
                        bank = 4 + (gi % 2) * 2 + k8
                        vector.wait_ge(TC.h, marks[k8])
                        kb = hs * 8 + k8 * 4
                        dstw = wtdst[:, 0:2048].rearrange(
                            "p (k o) -> p k o", o=128)[:, kb:kb + 4, :]
                        evac_mark[bank] = VC.inc(vector.tensor_copy(
                            dstw, pp[bank].rearrange("p (k o) -> p k o", o=128)))
                    if hs == 1:
                        tdst = [ternG_r, ternU_r][mi2]
                        wtd = [wtg, wtu][mi2]
                        gpsimd.wait_ge(VC.h, evac_mark[4 + (gi % 2) * 2 + 1])
                        tern_spill[ab] = gsts(
                            GT, tdst[a],
                            wtd[:, 0:2048].rearrange("p (k o) -> p k o", o=128))

                if a >= 2:
                    sync.wait_ge(TC.h, wt_free[ab])
                    sync.wait_ge(GT.h, tern_spill[ab])
                pend = None
                for gi in range(4):
                    marks = g_tern(gi)
                    if pend is not None:
                        g_evac(pend[0], pend[1])
                    pend = (gi, marks)
                g_evac(pend[0], pend[1])
                wt_ready[a] = VC.n
            else:
                if a >= 2:
                    sync.wait_ge(TC.h, wt_free[ab])
                lds(SWT[ab], wtg.rearrange("p (k o) -> p k o", o=128), ternG_r[a])
                wt_ready[a] = lds(SWT[ab], wtu.rearrange("p (k o) -> p k o", o=128),
                                  ternU_r[a])

        def c1_consume(a):
            ab = a % 2
            wtg = wtg2[ab]
            wtu = wtu2[ab]
            if hf == 0:
                tensor.wait_ge(VC.h, wt_ready[a])
            else:
                tensor.wait_ge(SWT[ab].h, wt_ready[a])
            bg, bu = ab, 2 + ab
            tensor.wait_ge(VC.h, bank_free[bg])
            tensor.wait_ge(VC.h, bank_free[bu])
            for k in range(16):
                mm = tensor.matmul(pp[bg],
                                   wtg[:, k * 128:(k + 1) * 128],
                                   xqT[:, k * TPC + t0: k * TPC + t0 + HALF],
                                   start=(k == 0), stop=(k == 15))
            TC.inc(mm)
            for k in range(16):
                mm = tensor.matmul(pp[bu],
                                   wtu[:, k * 128:(k + 1) * 128],
                                   xqT[:, k * TPC + t0: k * TPC + t0 + HALF],
                                   start=(k == 0), stop=(k == 15))
            wt_free[ab] = gu_mm[a] = TC.inc(mm)

        def c1_post(a):
            ab = a % 2
            bg, bu = ab, 2 + ab
            vector.wait_ge(TC.h, gu_mm[a])
            if a >= 1:
                vector.wait_ge(SC.h, sil_mark[a - 1])
            vector.tensor_tensor(gt1, pp[bg], deqg, ALU.mult)
            VC.inc(vector.engine_nop())
            scalar.wait_ge(VC.h, VC.n)
            if a >= 1:
                scalar.wait_ge(VC.h, inter_done[a - 1])
            sil_mark[a] = SC.inc(scalar.activation(c1silu, gt1, ACT.Silu))
            vector.tensor_tensor(c1up, pp[bu], dequ, ALU.mult)
            vector.wait_ge(SC.h, sil_mark[a])
            bank_free[bg] = bank_free[bu] = inter_done[a] = VC.inc(
                vector.tensor_tensor(inter[:, a * HALF:(a + 1) * HALF], c1silu, c1up,
                                     ALU.mult))

        def c1_h128(a):
            bh = 6 + a % 2
            tensor.wait_ge(VC.h, inter_done[a])
            tensor.wait_ge(VC.h, h128ev[bh])
            if hf == 0 and bh in evac_mark:
                tensor.wait_ge(VC.h, evac_mark[bh])
            h_mm = TC.inc(tensor.matmul(pp[bh], h128, inter[:, a * HALF:(a + 1) * HALF],
                                        start=True, stop=True))
            vector.wait_ge(TC.h, h_mm)
            h128ev[bh] = VC.inc(
                vector.tensor_copy(inter[:, a * HALF:(a + 1) * HALF], pp[bh]))

        for ci in range(NA + 3):
            if 1 <= ci <= NA:
                c1_consume(ci - 1)
            if ci < NA:
                c1_produce(ci)
            if 2 <= ci <= NA + 1:
                c1_post(ci - 2)
            if 3 <= ci <= NA + 2:
                c1_h128(ci - 3)
        if hf == 0:
            wd_r = wd_d.rearrange("(r p) o -> r p o", p=128)
            ternD_w = ternD_d.rearrange("(og t p) m -> og p t m", t=8, p=128)
            wd_slab_free = {0: GC.n, 1: GC.n}
            wd_pn_free = {0: TC.n, 1: TC.n}
            wd_stage_free = {0: GW.n, 1: GW.n}
            wd_evmk = {}
            wd_jobs = [(r, og) for r in range(16) for og in range(8)]

            def wd_tern(i):
                r, og = wd_jobs[i]
                sidx = i % 2
                sb = slab[sidx]
                if i < 2:
                    sync.wait_ge(GC.h, slab_free[sidx])
                else:
                    sync.wait_ge(GC.h, wd_slab_free[sidx])
                lv = lds(SSL[sidx], sb, wd_r[r, :, og * 1024:(og + 1) * 1024])
                scalar.wait_ge(SSL[sidx].h, lv)
                scalar.wait_ge(TC.h, wd_pn_free[sidx])
                scalar.activation(pnh[sidx], sb, ACT.Sign,
                                  scale=wspos[:, 2:3], bias=bm05[:, 0:1])
                smark = SC.inc(scalar.activation(sb, sb, ACT.Sign,
                                                 scale=wspos[:, 2:3], bias=bp05[:, 0:1]))
                gpsimd.wait_ge(SC.h, smark)
                gpsimd.wait_ge(TC.h, wd_pn_free[sidx])
                wd_slab_free[sidx] = GC.inc(
                    gpsimd.tensor_tensor(pnh[sidx], pnh[sidx], sb, ALU.add))
                tensor.wait_ge(GC.h, wd_slab_free[sidx])
                marks = []
                for k8 in range(2):
                    bank = 4 + (i % 2) * 2 + k8
                    if bank in wd_evmk:
                        tensor.wait_ge(SC.h, wd_evmk[bank])
                    else:
                        if bank in evac_mark:
                            tensor.wait_ge(VC.h, evac_mark[bank])
                        if bank in h128ev:
                            tensor.wait_ge(VC.h, h128ev[bank])
                    for kk in range(4):
                        k = k8 * 4 + kk
                        mm = tensor.transpose(pp[bank][:, kk * 128:(kk + 1) * 128],
                                              pnh[sidx][:, k * 128:(k + 1) * 128],
                                              identf)
                    marks.append(TC.inc(mm))
                wd_pn_free[sidx] = marks[-1]
                return marks

            def wd_evac(i, marks):
                r, og = wd_jobs[i]
                sidx = i % 2
                stg = 4096 + sidx * 1024
                scalar.wait_ge(GW.h, wd_stage_free[sidx])
                for k8 in range(2):
                    bank = 4 + (i % 2) * 2 + k8
                    scalar.wait_ge(TC.h, marks[k8])
                    wd_evmk[bank] = SC.inc(scalar.copy(
                        wtb[:, stg + k8 * 512: stg + (k8 + 1) * 512], pp[bank]))
                gpsimd.wait_ge(SC.h, wd_evmk[bank])
                wd_stage_free[sidx] = gsts(
                    GW, ternD_w[og][:, :, r * 128:(r + 1) * 128],
                    wtb[:, stg:stg + 1024].rearrange("p (t m) -> p t m", m=128))

            wd_pend = None
            for i in range(128):
                marks = wd_tern(i)
                if wd_pend is not None:
                    wd_evac(*wd_pend)
                wd_pend = (i, marks)
            wd_evac(*wd_pend)
        if debug and hf == 0:
            gpsimd.wait_ge(VC.h, VC.n)
            for a in range(NA):
                gst(dbg["dbg_int"][:, a * 128:(a + 1) * 128],
                    inter[:, a * HALF: a * HALF + 128])
            vector.wait_ge(G.h, G.n)

        for s in [1, 2, 4, 8, 16, 32]:
            if s == 1:
                for a in range(NA):
                    if a & s:
                        continue
                    b2 = a + s
                    Aap = inter[:, a * HALF:(a + 1) * HALF]
                    Bap = inter[:, b2 * HALF:(b2 + 1) * HALF]
                    vector.tensor_tensor(Aap, Aap, Bap, ALU.add)
                    vector.scalar_tensor_tensor(Bap, Bap, -2.0, Aap, ALU.mult, ALU.add)
            else:
                for a in range(0, NA, 2):
                    if a & s:
                        continue
                    b2 = a + s
                    Aap = inter[:, a * HALF:(a + 2) * HALF]
                    Bap = inter[:, b2 * HALF:(b2 + 2) * HALF]
                    vector.tensor_tensor(Aap, Aap, Bap, ALU.add)
                    vector.scalar_tensor_tensor(Bap, Bap, -2.0, Aap, ALU.mult, ALU.add)
        mminb = dqb[:, 1024:1536]
        vector.tensor_copy(mmaxb, inter[:, 0:HALF])
        vector.tensor_copy(mminb, inter[:, 0:HALF])
        for a in range(1, NA):
            vector.tensor_tensor(mmaxb, mmaxb, inter[:, a * HALF:(a + 1) * HALF], ALU.max)
            vector.tensor_tensor(mminb, mminb, inter[:, a * HALF:(a + 1) * HALF], ALU.min)
        VC.inc(vector.engine_nop())
        tensor.wait_ge(VC.h, VC.n)
        tensor.wait_ge(SC.h, SC.n)
        for j in range(NHT):
            mm = tensor.transpose(pp[6][:, j * 128:(j + 1) * 128],
                                  mmaxb[:, j * 128:(j + 1) * 128], identf)
            mm = tensor.transpose(pp[7][:, j * 128:(j + 1) * 128],
                                  mminb[:, j * 128:(j + 1) * 128], identf)
        TC.inc(mm)
        vector.wait_ge(TC.h, TC.n)
        vector.tensor_reduce(sm[:, 8:12], pp[6].rearrange("p (j n) -> p j n", n=128),
                             AX.X, ALU.max)
        VC.inc(vector.tensor_reduce(sm[:, 4:8], pp[7].rearrange("p (j n) -> p j n", n=128),
                                    AX.X, ALU.min))
        vector.wait_ge(VC.h, VC.n)
        vector.tensor_scalar(sm[:, 4:8], sm[:, 4:8], -1.0, None, ALU.mult)
        VC.inc(vector.engine_nop())
        vector.wait_ge(VC.h, VC.n)
        VC.inc(vector.tensor_tensor(sm[:, 8:12], sm[:, 8:12], sm[:, 4:8], ALU.max))
        gpsimd.wait_ge(VC.h, VC.n)
        GC.inc(gpsimd.tensor_scalar(amax2, sm[:, 8:12], 1e-5, None, ALU.max))
        vector.wait_ge(GC.h, GC.n)
        VC.inc(vector.reciprocal(sm[:, 8:12], amax2))
        scalar.wait_ge(VC.h, VC.n)
        SC.inc(scalar.copy(sm[:, 12:16], sm[:, 8:12]))
        vector.wait_ge(SC.h, SC.n)
        VC.inc(vector.tensor_scalar(sm[:, 12:16], sm[:, 12:16], 128.0, None, ALU.mult))
        vector.tensor_scalar(deq2, amax2, wmb[:, 2:3], 1.0 / 256.0, ALU.mult, ALU.mult)
        VC.inc(vector.engine_nop())
        gpsimd.wait_ge(VC.h, VC.n)
        g1 = GA.inc(gpsimd.dma_start(out=s2row_d.rearrange("o (c p) -> o p c", p=128)[0],
                                     in_=sm[:, 12:16], allow_slow_non_contiguous=True), 16)
        sync.wait_ge(GA.h, g1)
        lds(SDQ, s2bb, s2row_d.broadcast_to((128, HALF)))
        lv = lds(SDQ, cbuf[:, 2560:3072], s2row_d.broadcast_to((128, HALF)))
        vector.wait_ge(SDQ.h, lv)
        VC.inc(vector.engine_nop())
        s2bbW = cbuf[:, 2048:3072]
        d2w = pnb[:, 0:1024]
        mk2w = pnb[:, 1024:2048]

        def quantize_pair(a):
            pair = inter[:, a * HALF:(a + 2) * HALF]
            vector.tensor_tensor(pair, pair, s2bbW, ALU.mult)
            vector.tensor_scalar(pair, pair, 128.5, 255.4999, ALU.add, ALU.min)
            vector.tensor_copy(d2w, pair)
            for bit in SEARCH_BITS:
                vector.tensor_scalar(mk2w, d2w, bit, -bit, ALU.is_ge, ALU.mult)
                vector.tensor_tensor(d2w, d2w, mk2w, ALU.add)
            return VC.inc(vector.scalar_tensor_tensor(pair, pair, 128.0, d2w,
                                                      ALU.subtract, ALU.subtract))


        xq2b = wtb[:, 2048:2048 + 2 * 512]
        wds = wtb[:, 0:2 * 1024]
        barrier()
        cmb_marks = {}
        out_marks = {}
        cmb_ji = 0
        for p2 in range(2):
            m0 = p2 * 1024
            dn_mm = {0: TC.n, 1: TC.n}
            for a in range(NA):
                ab2 = a % 2
                if a >= 2:
                    sync.wait_ge(TC.h, dn_mm[ab2])
                lv = lds(SWD[ab2], wds[:, ab2 * 1024:(ab2 + 1) * 1024].rearrange(
                    "p (h n) -> p h n", h=1)[:, 0, :],
                    ternD_d[a * 128:(a + 1) * 128, m0:m0 + 1024])
                if p2 == 0 and a % 2 == 0:
                    quantize_pair(a)
                if a >= 2:
                    vector.wait_ge(TC.h, dn_mm[ab2])
                VC.inc(vector.tensor_copy(xq2b[:, ab2 * 512:(ab2 + 1) * 512],
                                          inter[:, a * HALF:(a + 1) * HALF]))
                tensor.wait_ge(SWD[ab2].h, lv)
                tensor.wait_ge(VC.h, VC.n)
                for tg in range(NHT):
                    for m2 in range(2):
                        mm = tensor.matmul(
                            pp[tg * 2 + m2],
                            xq2b[:, ab2 * 512 + tg * 128: ab2 * 512 + (tg + 1) * 128],
                            wds[:, ab2 * 1024 + m2 * 512: ab2 * 1024 + (m2 + 1) * 512],
                            start=(a == 0), stop=(a == NA - 1))
                dn_mm[ab2] = TC.inc(mm)
            scalar.wait_ge(TC.h, TC.n)
            sync.wait_ge(TC.h, TC.n)
            csetA = (gt1, gt2, gt3)
            csetB = (pnb[:, 0:512], cbuf[:, 2048:2560], cbuf[:, 2560:3072])
            csets = [csetA, csetB]
            cmb_jobs = [(tg, m2) for tg in range(NHT) for m2 in range(2)]
            for (tg, m2) in cmb_jobs:
                ji = cmb_ji
                cmb_ji += 1
                par = ji % 2
                cg1, cg2, cg3 = csets[par]
                mc = m0 + m2 * 512
                rowa = t0 + tg * 128
                if ji >= 2:
                    sync.wait_ge(VC.h, cmb_marks[ji - 2])
                lds(SCMB[par], cg2, x_d[rowa:rowa + 128, mc:mc + 512])
                ls = lds(SCMB[par], cg3, s_sp[rowa:rowa + 128, mc:mc + 512])
                if ji >= 2:
                    scalar.wait_ge(GOUT.h, out_marks[ji - 2])
                SC.inc(scalar.activation(cg1, pp[tg * 2 + m2], ACT.Silu,
                                         scale=deq2[:, tg:tg + 1]))
                vector.wait_ge(SC.h, SC.n)
                vector.wait_ge(SCMB[par].h, ls)
                vector.tensor_tensor(cg1, cg1, cg3, ALU.mult)
                cmb_marks[ji] = VC.inc(vector.tensor_tensor(cg1, cg1, cg2, ALU.add))
                gpsimd.wait_ge(VC.h, VC.n)
                out_marks[ji] = gsts(GOUT, out_d[rowa:rowa + 128, mc:mc + 512], cg1)

    if debug:
        barrier()
        for i in range(NT):
            for mb2 in range(4):
                lv = ld(gt2, s_sp[i * 128:(i + 1) * 128, mb2 * 512:(mb2 + 1) * 512])
                gpsimd.wait_ge(LD.h, lv)
                gm = gst(dbg["dbg_s2"][i * 128:(i + 1) * 128, mb2 * 512:(mb2 + 1) * 512], gt2)
                sync.wait_ge(G.h, gm)
    barrier()
    dq_t.__exit__(None, None, None)
    wt_t.__exit__(None, None, None)
    cbuf_t.__exit__(None, None, None)
    inter_t.__exit__(None, None, None)
    return nc, dbg


def _host_constants():
    identf = np.eye(128, dtype=np.float32)
    identb = np.eye(128, dtype=np.float32).astype(ml_dtypes.bfloat16)
    h = np.array([[1.0]], dtype=np.float64)
    while h.shape[0] < 128:
        h = np.block([[h, h], [h, -h]])
    c = np.float32(1.0 / np.sqrt(np.float64(8192.0)))
    h128 = (h.astype(np.float32) * c).astype(np.float32)
    m1 = np.full((128, 128), -1e9, dtype=np.float32)
    for t8 in range(8):
        for i in range(16):
            for j in range(i + 1):
                m1[t8 * 16 + i, t8 * 16 + j] = 0.0
    mask4 = np.concatenate([m1] * 4, axis=1)
    ones = np.ones((128, 128), dtype=np.float32)
    return identf, identb, h128, mask4, ones


def make_in_maps(inputs):
    x = np.ascontiguousarray(np.asarray(inputs["L0_attn_x"], dtype=np.float32).reshape(TOK, D))
    wqkv = np.ascontiguousarray(np.asarray(inputs["L0_attn_Wqkv"], dtype=np.float32).reshape(D, 6144))
    bq = np.ascontiguousarray(
        np.asarray(inputs["L0_attn_bqkv"], dtype=np.float32).reshape(6144).reshape(48, 128).T)
    wout = np.ascontiguousarray(np.asarray(inputs["L0_attn_Wout"], dtype=np.float32).reshape(D, D))
    bout = np.ascontiguousarray(np.asarray(inputs["L0_attn_bout"], dtype=np.float32).reshape(1, D))
    wg = np.ascontiguousarray(np.asarray(inputs["L0_ffn_w_gate"], dtype=np.float32))
    wu = np.ascontiguousarray(np.asarray(inputs["L0_ffn_w_up"], dtype=np.float32))
    wd = np.ascontiguousarray(np.asarray(inputs["L0_ffn_w_down"], dtype=np.float32))
    identf, identb, h128, mask4, ones = _host_constants()
    maps = []
    for c in range(NCORES):
        maps.append({
            "x": x[c * TPC:(c + 1) * TPC], "wqkv": wqkv, "bqkv": bq, "wout": wout,
            "bout": bout, "wg": wg, "wu": wu, "wd": wd, "identf": identf,
            "h128": h128, "mask4": mask4, "ones": ones,
        })
    return maps


def kernel(L0_attn_x, L0_attn_Wqkv, L0_attn_bqkv, L0_attn_Wout, L0_attn_bout,
           L0_ffn_w_gate, L0_ffn_w_up, L0_ffn_w_down):
    maps = make_in_maps(dict(
        L0_attn_x=L0_attn_x, L0_attn_Wqkv=L0_attn_Wqkv, L0_attn_bqkv=L0_attn_bqkv,
        L0_attn_Wout=L0_attn_Wout, L0_attn_bout=L0_attn_bout,
        L0_ffn_w_gate=L0_ffn_w_gate, L0_ffn_w_up=L0_ffn_w_up,
        L0_ffn_w_down=L0_ffn_w_down))
    nc, _ = build_nc(debug=False, upto="C")
    res = run_bass_kernel_spmd(nc, maps, core_ids=list(range(NCORES)))
    out = np.concatenate([np.asarray(r["out"]) for r in res.results], axis=0)
    return out.reshape(4, 2048, D)


# version 13
# speedup vs baseline: 1.0894x; 1.0224x over previous
import numpy as np
import ml_dtypes
import concourse.bass as bass
import concourse.mybir as mybir
from concourse.bass_utils import run_bass_kernel_spmd

F32 = mybir.dt.float32
BF16 = mybir.dt.bfloat16
ALU = mybir.AluOpType
ACT = mybir.ActivationFunctionType
AX = mybir.AxisListType

NCORES = 8
TOK = 8192
TPC = TOK // NCORES
D = 2048
HALF = 512
NT = TPC // 128
NHT = HALF // 128
IFF = 8192
NA = IFF // 128
INV_SQRT_D3 = float(np.float32(1.0 / np.sqrt(np.float32(128.0))))
SEARCH_BITS = [128.0, 64.0, 32.0, 16.0, 8.0, 4.0, 2.0, 1.0]


class Sem:
    __slots__ = ("h", "n")

    def __init__(self, nc, name):
        self.h = nc.alloc_semaphore(name)
        self.n = 0

    def inc(self, instr, k=1):
        instr.then_inc(self.h, k)
        self.n += k
        return self.n


def build_nc(debug=False, upto="C"):
    nc = bass.Bass("TRN2", target_bir_lowering=False,
                   detect_race_conditions=False)

    x_d = nc.dram_tensor("x", [TPC, D], F32, kind="ExternalInput").ap()
    wqkv_d = nc.dram_tensor("wqkv", [D, 6144], F32, kind="ExternalInput").ap()
    bqkv_d = nc.dram_tensor("bqkv", [128, 48], F32, kind="ExternalInput").ap()
    wout_d = nc.dram_tensor("wout", [D, D], F32, kind="ExternalInput").ap()
    bout_d = nc.dram_tensor("bout", [1, D], F32, kind="ExternalInput").ap()
    wg_d = nc.dram_tensor("wg", [IFF, D], F32, kind="ExternalInput").ap()
    wu_d = nc.dram_tensor("wu", [IFF, D], F32, kind="ExternalInput").ap()
    wd_d = nc.dram_tensor("wd", [D, IFF], F32, kind="ExternalInput").ap()
    identf_d = nc.dram_tensor("identf", [128, 128], F32, kind="ExternalInput").ap()
    h128_d = nc.dram_tensor("h128", [128, 128], F32, kind="ExternalInput").ap()
    mask4_d = nc.dram_tensor("mask4", [128, 512], F32, kind="ExternalInput").ap()
    ones_d = nc.dram_tensor("ones", [128, 128], F32, kind="ExternalInput").ap()
    out_d = nc.dram_tensor("out", [TPC, D], F32, kind="ExternalOutput").ap()
    dbg = {}
    if debug:
        for nm, shp, dt in [("dbg_xn", [TPC, D], F32), ("dbg_xq", [D, TPC], BF16),
                            ("dbg_rs", [128, 64], F32), ("dbg_sv", [128, 512], F32),
                            ("dbg_s", [TPC, D], F32), ("dbg_means", [128, 4], F32),
                            ("dbg_int", [128, IFF], F32), ("dbg_xq2", [128, IFF], F32),
                            ("dbg_sm", [128, 128], F32), ("dbg_s2", [TPC, D], F32)]:
            dbg[nm] = nc.dram_tensor(nm, shp, dt, kind="ExternalOutput").ap()

    s_sp = nc.dram_tensor("s_sp", [TPC, D], F32, kind="Internal").ap()
    rs_dram = nc.dram_tensor("rs_dram", [512, 16], F32, kind="Internal").ap()
    a1row_d = nc.dram_tensor("a1row", [1, TPC], F32, kind="Internal").ap()
    s2row_d = nc.dram_tensor("s2row", [1, HALF], F32, kind="Internal").ap()
    ternG_d = nc.dram_tensor("ternG", [NA * 16 * 128, 128], BF16, kind="Internal").ap()
    ternU_d = nc.dram_tensor("ternU", [NA * 16 * 128, 128], BF16, kind="Internal").ap()
    ternD_d = nc.dram_tensor("ternD", [IFF, D], BF16,
                             kind="ExternalOutput" if debug else "Internal").ap()

    LD = Sem(nc, "LD")
    G = Sem(nc, "G")
    Q = Sem(nc, "Q")
    TC = Sem(nc, "TC")
    VC = Sem(nc, "VC")
    SC = Sem(nc, "SC")
    GC = Sem(nc, "GC")
    SX = [Sem(nc, "SX0"), Sem(nc, "SX1")]
    SW = [Sem(nc, "SW0"), Sem(nc, "SW1"), Sem(nc, "SW2")]
    SBB = Sem(nc, "SBB")
    SSL = [Sem(nc, "SSL0"), Sem(nc, "SSL1")]
    SWT = [Sem(nc, "SWTa"), Sem(nc, "SWTb")]
    SWD = [Sem(nc, "SWD0"), Sem(nc, "SWD1")]
    SDQ = Sem(nc, "SDQ")
    SCMB = [Sem(nc, "SCMB0"), Sem(nc, "SCMB1")]
    QM = [Sem(nc, "QM%d" % i) for i in range(8)]
    GS = [Sem(nc, "GS0"), Sem(nc, "GS1")]
    GT = Sem(nc, "GT")
    GW = Sem(nc, "GW")
    GR = Sem(nc, "GR")
    GA = Sem(nc, "GA")
    GOUT = Sem(nc, "GOUT")
    ALL_SEMS = [LD, G, Q, TC, VC, SC, GC, SBB, SDQ, GT, GW, GR, GA, GOUT] + SX + SW + SSL + SWD + QM + GS + SWT + SCMB

    A = lambda name, shape, dt: nc.alloc_sbuf_tensor("sb_" + name, shape, dt).ap()
    identf = A("identf", [128, 128], F32)
    h128 = A("h128", [128, 128], F32)
    ones = A("ones", [128, 128], F32)
    amax1 = A("amax1", [128, NT], F32)
    sm = A("sm", [128, 16], F32)
    msums = A("msums", [128, 4], F32)
    msw = A("msw", [128, 768], F32)
    wmean = A("wmean", [128, 4], F32)
    wspos = A("wspos", [128, 4], F32)
    wsposb = A("wsposb", [128, 4], F32)
    wsneg = A("wsneg", [128, 4], F32)
    wmb = A("wmb", [128, 4], F32)
    m4 = A("m4", [128, 4], F32)
    m4n = A("m4n", [128, 4], F32)
    s4 = A("s4", [128, 4], F32)
    amax2 = A("amax2", [128, 4], F32)
    deq2 = A("deq2", [128, 4], F32)
    bm05 = A("bm05", [128, 1], F32)
    bp05 = A("bp05", [128, 1], F32)
    xqT = A("xqT", [128, 16 * TPC], BF16)

    pp = [nc.alloc_psum_tensor("pp%d" % i, [128, 512], F32).ap() for i in range(8)]

    sync, tensor, vector, scalar, gpsimd = nc.sync, nc.tensor, nc.vector, nc.scalar, nc.gpsimd

    def ld(dst, src):
        LD.inc(sync.dma_start(out=dst, in_=src), 16)
        return LD.n

    def gst(dst, src):
        G.inc(gpsimd.dma_start(out=dst, in_=src), 16)
        return G.n

    def lds(sem, dst, src2, **kw):
        sem.inc(sync.dma_start(out=dst, in_=src2, **kw), 16)
        return sem.n

    def gsts(sem, dst, src2, **kw):
        sem.inc(gpsimd.dma_start(out=dst, in_=src2, **kw), 16)
        return sem.n

    def barrier():
        VC.inc(vector.engine_nop())
        SC.inc(scalar.copy(sm[:, 15:16], sm[:, 15:16]))
        for eng in (sync, tensor, vector, scalar, gpsimd):
            for s in ALL_SEMS:
                if s.n:
                    eng.wait_ge(s.h, s.n)

    ld(identf, identf_d)
    ld(h128, h128_d)
    ld(ones, ones_d)
    vector.wait_ge(LD.h, LD.n)
    vector.memset(sm[:, 5:6], 1e-5)
    vector.memset(sm[:, 15:16], 0.0)
    vector.memset(bm05, -0.5)
    vector.memset(bp05, 0.5)
    VC.inc(vector.memset(msums, 0.0))
    scalar.wait_ge(VC.h, VC.n)

    mch_t = nc.sbuf_tensor("mchunk", [128, 3072], F32)
    mchunk = mch_t.__enter__().ap()
    means_jobs = []
    for mi, wsrc in enumerate([wg_d, wu_d, wd_d]):
        flat = wsrc.rearrange("a b -> (a b)").rearrange("(p n) -> p n", p=128)
        for j in range(256):
            means_jobs.append((mi, flat[:, j * 512:(j + 1) * 512]))
    mst = {"issued": 0, "done": 0}
    means_abs = {}
    means_marks = {}

    def means_issue():
        i = mst["issued"]
        if i >= len(means_jobs):
            return
        _, src = means_jobs[i]
        buf = mchunk[:, (i % 6) * 512:((i % 6) + 1) * 512]
        if i >= 6:
            scalar.wait_ge(SC.h, means_abs[i - 6])
        means_marks[i] = QM[i % 6].inc(scalar.dma_start(out=buf, in_=src), 16)
        mst["issued"] = i + 1

    def means_step(k=1):
        for _ in range(k):
            i = mst["done"]
            if i >= len(means_jobs):
                return
            while mst["issued"] < min(i + 6, len(means_jobs)):
                means_issue()
            mi, _ = means_jobs[i]
            buf = mchunk[:, (i % 6) * 512:((i % 6) + 1) * 512]
            scalar.wait_ge(QM[i % 6].h, means_marks[i])
            means_abs[i] = SC.inc(
                scalar.activation(buf, buf, ACT.Abs, accum_out=msw[:, i:i + 1]))
            mst["done"] = i + 1

    xnT_t = nc.sbuf_tensor("xnT", [128, 16 * TPC], F32)
    xnT = xnT_t.__enter__().ap()
    workA_t = nc.sbuf_tensor("workA", [128, 4 * D], F32)
    workA = workA_t.__enter__().ap()

    xb = [workA[:, 0:D], workA[:, D:2 * D]]
    xnb = workA[:, 2 * D:3 * D]
    mkb = workA[:, 3 * D:4 * D]

    vc_tile = [0, 0]
    evac_mark = {}
    for i in range(NT):
        b = i % 2
        if vc_tile[b]:
            sync.wait_ge(VC.h, vc_tile[b])
        lv = lds(SX[b], xb[b], x_d[i * 128:(i + 1) * 128, :])
        vector.wait_ge(SX[b].h, lv)
        VC.inc(vector.tensor_reduce(sm[:, 0:1], xb[b], AX.X, ALU.add))
        gpsimd.wait_ge(VC.h, VC.n)
        GC.inc(gpsimd.tensor_scalar(sm[:, 1:2], sm[:, 0:1], -1.0 / 2048.0, None, ALU.mult))
        scalar.wait_ge(GC.h, GC.n)
        SC.inc(scalar.activation(xnb, xb[b], ACT.Identity, bias=sm[:, 1:2], scale=1.0))
        vector.wait_ge(SC.h, SC.n)
        vector.tensor_tensor(mkb, xnb, xnb, ALU.mult)
        VC.inc(vector.tensor_reduce(sm[:, 2:3], mkb, AX.X, ALU.add))
        scalar.wait_ge(VC.h, VC.n)
        SC.inc(scalar.activation(sm[:, 3:4], sm[:, 2:3], ACT.Sqrt, bias=sm[:, 5:6],
                                 scale=1.0 / 2048.0))
        vector.wait_ge(SC.h, SC.n)
        VC.inc(vector.reciprocal(sm[:, 4:5], sm[:, 3:4]))
        gpsimd.wait_ge(VC.h, VC.n)
        GC.inc(gpsimd.tensor_copy(sm[:, 12:13], sm[:, 4:5]))
        vector.wait_ge(GC.h, GC.n)
        vector.tensor_scalar(xnb, xnb, sm[:, 12:13], None, ALU.mult)
        VC.inc(vector.tensor_reduce(sm[:, 13:14], xnb, AX.X, ALU.max,
                                    apply_absolute_value=True))
        gpsimd.wait_ge(VC.h, VC.n)
        GC.inc(gpsimd.tensor_scalar(amax1[:, i:i + 1], sm[:, 13:14], 1e-5, None, ALU.max))
        vector.wait_ge(GC.h, GC.n)
        VC.inc(vector.reciprocal(sm[:, 6:7], amax1[:, i:i + 1]))
        scalar.wait_ge(VC.h, VC.n)
        SC.inc(scalar.copy(sm[:, 7:8], sm[:, 6:7]))
        if debug:
            gpsimd.wait_ge(SC.h, SC.n)
            gdm = gst(dbg["dbg_sm"][:, i * 16:(i + 1) * 16], sm)
            vector.wait_ge(G.h, gdm)
        tensor.wait_ge(VC.h, VC.n)
        for k4 in range(4):
            bank = k4 % 2
            if bank in evac_mark:
                tensor.wait_ge(VC.h, evac_mark[bank])
            for kk in range(4):
                k = k4 * 4 + kk
                mm = tensor.transpose(pp[bank][:, kk * 128:(kk + 1) * 128],
                                      xnb[:, k * 128:(k + 1) * 128], identf)
            TC.inc(mm)
            vector.wait_ge(TC.h, TC.n)
            dst = xnT.rearrange("p (k t) -> p k t", t=TPC)[:, k4 * 4:(k4 + 1) * 4,
                                                          i * 128:(i + 1) * 128]
            evac_mark[bank] = VC.inc(
                vector.tensor_copy(dst, pp[bank].rearrange("p (k t) -> p k t", t=128)))
        if debug:
            gpsimd.wait_ge(VC.h, VC.n)
            gm = gst(dbg["dbg_xn"][i * 128:(i + 1) * 128, :], xnb)
            vector.wait_ge(G.h, gm)
        u, dd = xb[b], xnb
        vector.wait_ge(SC.h, SC.n)
        vector.tensor_scalar(u, dd, sm[:, 7:8], 128.0, ALU.mult, ALU.mult)
        vector.tensor_scalar(u, u, 128.5, 255.4999, ALU.add, ALU.min)
        vector.tensor_copy(dd, u)
        for bit in SEARCH_BITS:
            vector.tensor_scalar(mkb, dd, bit, -bit, ALU.is_ge, ALU.mult)
            vector.tensor_tensor(dd, dd, mkb, ALU.add)
        VC.inc(vector.scalar_tensor_tensor(mkb, u, 128.0, dd, ALU.subtract, ALU.subtract))
        vc_tile[b] = VC.n
        tensor.wait_ge(VC.h, VC.n)
        for k4 in range(4):
            bank = 2 + (k4 % 2)
            if bank in evac_mark:
                tensor.wait_ge(VC.h, evac_mark[bank])
            for kk in range(4):
                k = k4 * 4 + kk
                mm = tensor.transpose(pp[bank][:, kk * 128:(kk + 1) * 128],
                                      mkb[:, k * 128:(k + 1) * 128], identf)
            TC.inc(mm)
            vector.wait_ge(TC.h, TC.n)
            dstq = xqT.rearrange("p (k t) -> p k t", t=TPC)[:, k4 * 4:(k4 + 1) * 4,
                                                           i * 128:(i + 1) * 128]
            evac_mark[bank] = VC.inc(
                vector.tensor_copy(dstq, pp[bank].rearrange("p (k t) -> p k t", t=128)))
        means_step(8)

    if debug:
        gpsimd.wait_ge(VC.h, VC.n)
        for k in range(16):
            gst(dbg["dbg_xq"][k * 128:(k + 1) * 128, :], xqT[:, k * TPC:(k + 1) * TPC])

    gpsimd.wait_ge(GC.h, GC.n)
    GA.inc(gpsimd.dma_start(out=a1row_d.rearrange("o (c p) -> o p c", p=128)[0],
                            in_=amax1, allow_slow_non_contiguous=True), 16)

    if upto == "A":
        barrier()
        return nc, dbg

    workA_t.__exit__(None, None, None)
    wqs_t = nc.sbuf_tensor("wqs", [128, 3 * 1024], F32)
    wqs = wqs_t.__enter__().ap()
    qkT_t = nc.sbuf_tensor("qkT", [128, 32 * HALF], F32)
    qkT = qkT_t.__enter__().ap()
    workB_t = nc.sbuf_tensor("workB", [128, 1024], F32)
    workB = workB_t.__enter__().ap()
    bsm_t = nc.sbuf_tensor("bsm", [128, 2048 + 1712], F32)
    bsm = bsm_t.__enter__().ap()
    rs_all = bsm[:, 0:64]
    rs_t = bsm[:, 64:128]
    sall = bsm[:, 1536 + 256:1536 + 320]
    sallb = bsm[:, 1536 + 320:1536 + 384]
    rall = bsm[:, 1536 + 384:1536 + 448]
    rallb = bsm[:, 1536 + 448:1536 + 512]
    mask4 = bsm[:, 2048:2560]
    scb = bsm[:, 2560:3584]
    t1sb = bsm[:, 3584:3712]
    bqkv = bsm[:, 3712:3760]
    ld(mask4, mask4_d)
    ld(bqkv, bqkv_d)
    svT = bsm[:, 128:128 + HALF]
    sv_t = bsm[:, 128 + HALF:128 + 2 * HALF]
    bb = bsm[:, 128 + 2 * HALF:128 + 2 * HALF + 512]
    attn_s = qkT[:, 8192:8192 + D]
    sst = [workB[:, 0:512], workB[:, 512:1024]]


    wq_r = wqkv_d.rearrange("(k p) n -> p k n", p=128)
    barrier()

    for hf in range(2):
        t0 = hf * HALF
        tensor.wait_ge(VC.h, VC.n)
        tensor.wait_ge(SC.h, SC.n)
        sync.wait_ge(TC.h, TC.n)
        VC.inc(vector.memset(svT, 0.0))
        evac_q = {0: SC.n, 1: SC.n}
        evac_v = {0: VC.n, 1: VC.n}
        mm_done = {0: TC.n, 1: TC.n}
        slot_mm = {0: TC.n, 1: TC.n, 2: TC.n}
        for c in range(48):
            b = c % 2
            if c >= 2:
                tensor.wait_ge(SC.h, evac_q[b])
                tensor.wait_ge(VC.h, evac_v[b])
            for hs in range(2):
                s4i = (2 * c + hs) % 3
                sync.wait_ge(TC.h, slot_mm[s4i])
                dstw = wqs[:, s4i * 1024:(s4i + 1) * 1024].rearrange("p (k c) -> p k c", c=128)
                lv = lds(SW[s4i], dstw, wq_r[:, 8 * hs:8 * hs + 8, c * 128:(c + 1) * 128])
                tensor.wait_ge(SW[s4i].h, lv)
                for kk in range(8):
                    k = 8 * hs + kk
                    mm = tensor.matmul(pp[b],
                                       wqs[:, s4i * 1024 + kk * 128: s4i * 1024 + (kk + 1) * 128],
                                       xnT[:, k * TPC + t0: k * TPC + t0 + HALF],
                                       start=(k == 0), stop=(k == 15))
                slot_mm[s4i] = TC.inc(mm)
            mm_done[b] = slot_mm[(2 * c + 1) % 3]
            h = c // 3
            part = c % 3
            if part < 2:
                dstqk = qkT.rearrange("p (r t h2) -> p r t h2", r=2, h2=16)[:, part, :, h]
                scalar.wait_ge(TC.h, TC.n)
                evac_q[b] = SC.inc(
                    scalar.activation(dstqk, pp[b],
                                      ACT.Identity, bias=bqkv[:, c:c + 1], scale=1.0))
            else:
                vector.wait_ge(TC.h, TC.n)
                evac_v[b] = VC.inc(
                    vector.scalar_tensor_tensor(svT, pp[b], bqkv[:, c:c + 1], svT,
                                                ALU.add, ALU.add))
            means_step(5)
        if upto == "B1":
            break
        scalar.wait_ge(VC.h, VC.n)
        sc_bank_free = {2: VC.n, 3: VC.n}
        for gb in range(16):
            bank = 2 + gb % 2
            if gb == 0:
                tensor.wait_ge(SC.h, SC.n)
            tensor.wait_ge(VC.h, sc_bank_free[bank])
            for q4 in range(4):
                g = gb * 4 + q4
                qap = qkT[:, g * 128:(g + 1) * 128]
                kap = qkT[:, 8192 + g * 128: 8192 + (g + 1) * 128]
                mm = tensor.matmul(pp[bank][:, q4 * 128:(q4 + 1) * 128], qap, kap,
                                   start=True, stop=True)
            TC.inc(mm)
            vector.wait_ge(TC.h, TC.n)
            vector.scalar_tensor_tensor(scb[:, 0:512], pp[bank], INV_SQRT_D3, mask4,
                                        ALU.mult, ALU.add)
            sc_bank_free[bank] = VC.inc(vector.engine_nop())
            VC.inc(vector.tensor_reduce(m4, scb[:, 0:512].rearrange("p (g n) -> p g n", n=128),
                                         AX.X, ALU.max))
            gpsimd.wait_ge(VC.h, VC.n)
            GC.inc(gpsimd.tensor_scalar(m4n, m4, -1.0, None, ALU.mult))
            scalar.wait_ge(GC.h, GC.n)
            for q4 in range(4):
                mm2 = scalar.activation(scb[:, 512 + q4 * 128: 512 + (q4 + 1) * 128],
                                        scb[:, q4 * 128:(q4 + 1) * 128], ACT.Exp,
                                        bias=m4n[:, q4:q4 + 1], scale=1.0)
            SC.inc(mm2)
            vector.wait_ge(SC.h, SC.n)
            VC.inc(vector.tensor_reduce(sall[:, gb * 4:(gb + 1) * 4],
                                        scb[:, 512:1024].rearrange("p (g n) -> p g n", n=128),
                                        AX.X, ALU.add))
            means_step(2)
        if upto == "B2":
            break
        gpsimd.wait_ge(VC.h, VC.n)
        GC.inc(gpsimd.tensor_copy(sallb, sall))
        vector.wait_ge(GC.h, GC.n)
        VC.inc(vector.reciprocal(rall, sallb))
        scalar.wait_ge(VC.h, VC.n)
        SC.inc(scalar.copy(rallb, rall))
        vector.wait_ge(SC.h, SC.n)
        VC.inc(vector.tensor_tensor(rs_all[:, 0:64], sallb, rallb, ALU.mult))
        tensor.wait_ge(VC.h, VC.n)
        TC.inc(tensor.transpose(pp[4][0:64, 0:128], rs_all[:, 0:64], identf))
        vector.wait_ge(TC.h, TC.n)
        VC.inc(vector.tensor_copy(t1sb[0:64, :], pp[4][0:64, 0:128]))
        gpsimd.wait_ge(VC.h, VC.n)
        g1 = gsts(GR, rs_dram.rearrange("(g r) i -> g (r i)", r=8), t1sb[0:64, 0:128])
        gpsimd.wait_ge(GR.h, g1)
        rs_sh = gsts(GR, rs_t[:, 0:64].rearrange("p (j i) -> p j i", i=16),
                     rs_dram.rearrange("(j p) i -> p j i", p=128))
        for j in range(NHT):
            mm = tensor.transpose(pp[5][:, j * 128:(j + 1) * 128],
                                  svT[:, j * 128:(j + 1) * 128], identf)
        TC.inc(mm)
        vector.wait_ge(TC.h, TC.n)
        VC.inc(vector.tensor_copy(sv_t, pp[5]))
        if debug and hf == 0:
            gpsimd.wait_ge(VC.h, VC.n)
            gpsimd.wait_ge(G.h, rs_sh)
            gst(dbg["dbg_rs"], rs_t)
            gst(dbg["dbg_sv"], sv_t)
        if upto == "B3":
            break
        attnT = qkT[:, 0:16 * HALF]
        vector.wait_ge(GR.h, rs_sh)
        for j in range(NHT):
            for i in range(16):
                vector.tensor_scalar(attn_s[:, i * 128:(i + 1) * 128],
                                     sv_t[:, j * 128:(j + 1) * 128],
                                     rs_t[:, j * 16 + i:j * 16 + i + 1], None, ALU.mult)
            VC.inc(vector.engine_nop())
            tensor.wait_ge(VC.h, VC.n)
            for c4 in range(4):
                bank = 6 + (c4 % 2)
                if bank in evac_mark:
                    tensor.wait_ge(VC.h, evac_mark[bank])
                for cc in range(4):
                    c = c4 * 4 + cc
                    mm = tensor.transpose(pp[bank][:, cc * 128:(cc + 1) * 128],
                                          attn_s[:, c * 128:(c + 1) * 128], identf)
                TC.inc(mm)
                vector.wait_ge(TC.h, TC.n)
                dsta = attnT.rearrange("p (c t) -> p c t", t=HALF)[:, c4 * 4:(c4 + 1) * 4,
                                                                  j * 128:(j + 1) * 128]
                evac_mark[bank] = VC.inc(
                    vector.tensor_copy(dsta, pp[bank].rearrange("p (c t) -> p c t", t=128)))
            means_step(6)
        if upto == "B5":
            break
        s_done = [0, 0]
        bb_free = 0
        wout_mm = {0: TC.n, 1: TC.n}
        evac_vc = VC.n
        for mb in range(4):
            if bb_free:
                sync.wait_ge(VC.h, bb_free)
            bb_done = lds(SBB, bb, bout_d[:, mb * 512:(mb + 1) * 512].broadcast_to((128, 512)))
            tensor.wait_ge(VC.h, evac_vc)
            if mb == 0:
                tensor.wait_ge(SC.h, SC.n)
            for hd in range(16):
                b = hd % 2
                sync.wait_ge(TC.h, wout_mm[b])
                lv = lds(SW[b], wqs[:, b * 1024: b * 1024 + 512],
                         wout_d[hd * 128:(hd + 1) * 128, mb * 512:(mb + 1) * 512])
                tensor.wait_ge(SW[b].h, lv)
                for j in range(NHT):
                    mm = tensor.matmul(pp[j],
                                       attnT[:, hd * HALF + j * 128: hd * HALF + (j + 1) * 128],
                                       wqs[:, b * 1024: b * 1024 + 512],
                                       start=(hd == 0), stop=(hd == 15))
                wout_mm[b] = TC.inc(mm)
            vector.wait_ge(TC.h, TC.n)
            vector.wait_ge(SBB.h, bb_done)
            for j in range(NHT):
                bs = j % 2
                if s_done[bs]:
                    vector.wait_ge(GS[bs].h, s_done[bs])
                bb_free = VC.inc(vector.tensor_tensor(sst[bs], pp[j], bb, ALU.add))
                gpsimd.wait_ge(VC.h, VC.n)
                s_done[bs] = gsts(
                    GS[bs],
                    s_sp[t0 + j * 128: t0 + (j + 1) * 128, mb * 512:(mb + 1) * 512],
                    sst[bs])
                if debug:
                    s_done[bs] = gsts(
                        GS[bs],
                        dbg["dbg_s"][t0 + j * 128: t0 + (j + 1) * 128,
                                     mb * 512:(mb + 1) * 512], sst[bs])
            evac_vc = VC.n
            means_step(8)
    means_step(len(means_jobs))

    barrier()
    bsm_t.__exit__(None, None, None)
    workB_t.__exit__(None, None, None)
    qkT_t.__exit__(None, None, None)
    wqs_t.__exit__(None, None, None)
    xnT_t.__exit__(None, None, None)
    mch_t.__exit__(None, None, None)

    if upto.startswith("B"):
        return nc, dbg

    vector.wait_ge(SC.h, SC.n)
    vector.memset(msums, 0.0)
    for mi3 in range(3):
        vector.tensor_reduce(msums[:, mi3:mi3 + 1], msw[:, mi3 * 256:(mi3 + 1) * 256],
                             AX.X, ALU.add)
    VC.inc(vector.engine_nop())
    tensor.wait_ge(VC.h, VC.n)
    TC.inc(tensor.matmul(pp[4][:, 0:4], ones, msums, start=True, stop=True))
    vector.wait_ge(TC.h, TC.n)
    VC.inc(vector.tensor_scalar(wmean[:, 0:4], pp[4][:, 0:4], 1.0 / 16777216.0, 1e-5,
                                ALU.mult, ALU.max))
    gpsimd.wait_ge(VC.h, VC.n)
    GC.inc(gpsimd.tensor_copy(wmb[:, 0:4], wmean[:, 0:4]))
    vector.wait_ge(GC.h, GC.n)
    VC.inc(vector.reciprocal(wspos[:, 0:4], wmb[:, 0:4]))
    gpsimd.wait_ge(VC.h, VC.n)
    GC.inc(gpsimd.tensor_scalar(wsneg[:, 0:4], wspos[:, 0:4], -1.0, None, ALU.mult))
    scalar.wait_ge(VC.h, VC.n)
    SC.inc(scalar.copy(wsposb[:, 0:4], wspos[:, 0:4]))
    if debug:
        gpsimd.wait_ge(VC.h, VC.n)
        gst(dbg["dbg_means"], wmb)

    inter_t = nc.sbuf_tensor("inter", [128, NA * HALF], F32)
    inter = inter_t.__enter__().ap()
    cbuf_t = nc.sbuf_tensor("cbuf", [128, 2 * 1024 + 1024 + 2048], F32)
    cbuf = cbuf_t.__enter__().ap()
    wt_t = nc.sbuf_tensor("wt", [128, 8 * 1024], BF16)
    wtb = wt_t.__enter__().ap()
    dq_t = nc.sbuf_tensor("dq", [128, 3 * 512], F32)
    dqb = dq_t.__enter__().ap()

    slab = [cbuf[:, i * 1024:(i + 1) * 1024] for i in range(2)]
    s2bb = cbuf[:, 2048:2560]
    mmaxb = cbuf[:, 2560:3072]
    pnb = cbuf[:, 3072:5120]
    wtg2 = [wtb[:, 0:2048], wtb[:, 2048:4096]]
    wtu2 = [wtb[:, 4096:6144], wtb[:, 6144:8192]]
    deqg = dqb[:, 0:512]
    dequ = dqb[:, 512:1024]
    gt1 = dqb[:, 1024:1536]
    gt2 = deqg
    gt3 = dequ

    wg_r = wg_d.rearrange("(a p) h -> a p h", p=128)
    wu_r = wu_d.rearrange("(a p) h -> a p h", p=128)
    ternG_r = ternG_d.rearrange("(a k p) o -> a p k o", k=16, p=128)
    ternU_r = ternU_d.rearrange("(a k p) o -> a p k o", k=16, p=128)
    barrier()

    for hf in range(2):
        t0 = hf * HALF
        if hf == 1:
            for eng in (vector, scalar, gpsimd, tensor, sync):
                eng.wait_ge(GOUT.h, GOUT.n)
                eng.wait_ge(SCMB[0].h, SCMB[0].n)
                eng.wait_ge(SCMB[1].h, SCMB[1].n)
        lv = lds(SDQ, deqg, a1row_d[:, t0:t0 + HALF].broadcast_to((128, HALF)))
        lv = lds(SDQ, dequ, a1row_d[:, t0:t0 + HALF].broadcast_to((128, HALF)))
        vector.wait_ge(SDQ.h, lv)
        vector.tensor_scalar(deqg, deqg, wmb[:, 0:1], 1.0 / 256.0, ALU.mult, ALU.mult)
        VC.inc(vector.tensor_scalar(dequ, dequ, wmb[:, 1:2], 1.0 / 256.0,
                                    ALU.mult, ALU.mult))

        slab_free = {i: GC.n for i in range(2)}
        h128ev = {6: VC.n, 7: VC.n}
        pnh = [pnb[:, 0:1024], pnb[:, 1024:2048]]
        pn_free = {0: TC.n, 1: TC.n}
        hsc_state = {"c": 0}
        wt_free = {0: TC.n, 1: TC.n}
        bank_free = {0: VC.n, 1: VC.n, 2: VC.n, 3: VC.n}
        tern_spill = {0: G.n, 1: G.n}
        wt_ready = {}
        gu_mm = {}
        inter_done = {}
        sil_mark = {}
        c1silu = cbuf[:, 2048:2560]
        c1up = cbuf[:, 2560:3072]

        def c1_produce(a):
            ab = a % 2
            wtg = wtg2[ab]
            wtu = wtu2[ab]
            if hf == 0:
                groups = []
                for mi2, (wsrc, tdst, wtdst, scol) in enumerate(
                        [(wg_r, ternG_r, wtg, 0), (wu_r, ternU_r, wtu, 1)]):
                    for hs in range(2):
                        groups.append((mi2, wsrc, wtdst, scol, hs))

                def g_tern(gi):
                    mi2, wsrc, wtdst, scol, hs = groups[gi]
                    hc = hsc_state["c"] % 2
                    hsc_state["c"] += 1
                    sb = slab[hs]
                    sync.wait_ge(GC.h, slab_free[hs])
                    lv = lds(SSL[hs], sb, wsrc[a, :, hs * 1024:(hs + 1) * 1024])
                    scalar.wait_ge(SSL[hs].h, lv)
                    scalar.wait_ge(TC.h, pn_free[hc])
                    scalar.activation(pnh[hc], sb, ACT.Sign,
                                      scale=wspos[:, scol:scol + 1], bias=bm05[:, 0:1])
                    smark = SC.inc(scalar.activation(
                        sb, sb, ACT.Sign,
                        scale=wspos[:, scol:scol + 1], bias=bp05[:, 0:1]))
                    gpsimd.wait_ge(SC.h, smark)
                    slab_free[hs] = GC.inc(
                        gpsimd.tensor_tensor(pnh[hc], pnh[hc], sb, ALU.add))
                    tensor.wait_ge(GC.h, slab_free[hs])
                    marks = []
                    for k8 in range(2):
                        bank = 4 + (gi % 2) * 2 + k8
                        if bank in evac_mark:
                            tensor.wait_ge(VC.h, evac_mark[bank])
                        if bank in h128ev:
                            tensor.wait_ge(VC.h, h128ev[bank])
                        for kk in range(4):
                            k = k8 * 4 + kk
                            mm = tensor.transpose(pp[bank][:, kk * 128:(kk + 1) * 128],
                                                  pnh[hc][:, k * 128:(k + 1) * 128],
                                                  identf)
                        marks.append(TC.inc(mm))
                    pn_free[hc] = marks[-1]
                    return marks

                def g_evac(gi, marks):
                    mi2, wsrc, wtdst, scol, hs = groups[gi]
                    if a >= 2 and gi == 0:
                        vector.wait_ge(TC.h, wt_free[ab])
                        vector.wait_ge(GT.h, tern_spill[ab])
                    for k8 in range(2):
                        bank = 4 + (gi % 2) * 2 + k8
                        vector.wait_ge(TC.h, marks[k8])
                        kb = hs * 8 + k8 * 4
                        dstw = wtdst[:, 0:2048].rearrange(
                            "p (k o) -> p k o", o=128)[:, kb:kb + 4, :]
                        evac_mark[bank] = VC.inc(vector.tensor_copy(
                            dstw, pp[bank].rearrange("p (k o) -> p k o", o=128)))
                    if hs == 1:
                        tdst = [ternG_r, ternU_r][mi2]
                        wtd = [wtg, wtu][mi2]
                        gpsimd.wait_ge(VC.h, evac_mark[4 + (gi % 2) * 2 + 1])
                        tern_spill[ab] = gsts(
                            GT, tdst[a],
                            wtd[:, 0:2048].rearrange("p (k o) -> p k o", o=128))

                if a >= 2:
                    sync.wait_ge(TC.h, wt_free[ab])
                    sync.wait_ge(GT.h, tern_spill[ab])
                pend = None
                for gi in range(4):
                    marks = g_tern(gi)
                    if pend is not None:
                        g_evac(pend[0], pend[1])
                    pend = (gi, marks)
                g_evac(pend[0], pend[1])
                wt_ready[a] = VC.n
            else:
                if a >= 2:
                    sync.wait_ge(TC.h, wt_free[ab])
                lds(SWT[ab], wtg.rearrange("p (k o) -> p k o", o=128), ternG_r[a])
                wt_ready[a] = lds(SWT[ab], wtu.rearrange("p (k o) -> p k o", o=128),
                                  ternU_r[a])

        def c1_consume(a):
            ab = a % 2
            wtg = wtg2[ab]
            wtu = wtu2[ab]
            if hf == 0:
                tensor.wait_ge(VC.h, wt_ready[a])
            else:
                tensor.wait_ge(SWT[ab].h, wt_ready[a])
            bg, bu = ab, 2 + ab
            tensor.wait_ge(VC.h, bank_free[bg])
            tensor.wait_ge(VC.h, bank_free[bu])
            for k in range(16):
                mm = tensor.matmul(pp[bg],
                                   wtg[:, k * 128:(k + 1) * 128],
                                   xqT[:, k * TPC + t0: k * TPC + t0 + HALF],
                                   start=(k == 0), stop=(k == 15))
            TC.inc(mm)
            for k in range(16):
                mm = tensor.matmul(pp[bu],
                                   wtu[:, k * 128:(k + 1) * 128],
                                   xqT[:, k * TPC + t0: k * TPC + t0 + HALF],
                                   start=(k == 0), stop=(k == 15))
            wt_free[ab] = gu_mm[a] = TC.inc(mm)

        def c1_post(a):
            ab = a % 2
            bg, bu = ab, 2 + ab
            vector.wait_ge(TC.h, gu_mm[a])
            if a >= 1:
                vector.wait_ge(SC.h, sil_mark[a - 1])
            vector.tensor_tensor(gt1, pp[bg], deqg, ALU.mult)
            VC.inc(vector.engine_nop())
            scalar.wait_ge(VC.h, VC.n)
            if a >= 1:
                scalar.wait_ge(VC.h, inter_done[a - 1])
            sil_mark[a] = SC.inc(scalar.activation(c1silu, gt1, ACT.Silu))
            vector.tensor_tensor(c1up, pp[bu], dequ, ALU.mult)
            vector.wait_ge(SC.h, sil_mark[a])
            bank_free[bg] = bank_free[bu] = inter_done[a] = VC.inc(
                vector.tensor_tensor(inter[:, a * HALF:(a + 1) * HALF], c1silu, c1up,
                                     ALU.mult))

        def c1_h128(a):
            bh = 6 + a % 2
            tensor.wait_ge(VC.h, inter_done[a])
            tensor.wait_ge(VC.h, h128ev[bh])
            if hf == 0 and bh in evac_mark:
                tensor.wait_ge(VC.h, evac_mark[bh])
            h_mm = TC.inc(tensor.matmul(pp[bh], h128, inter[:, a * HALF:(a + 1) * HALF],
                                        start=True, stop=True))
            vector.wait_ge(TC.h, h_mm)
            h128ev[bh] = VC.inc(
                vector.tensor_copy(inter[:, a * HALF:(a + 1) * HALF], pp[bh]))

        for ci in range(NA + 3):
            if 1 <= ci <= NA:
                c1_consume(ci - 1)
            if ci < NA:
                c1_produce(ci)
            if 2 <= ci <= NA + 1:
                c1_post(ci - 2)
            if 3 <= ci <= NA + 2:
                c1_h128(ci - 3)
        if hf == 0:
            wd_r = wd_d.rearrange("(r p) o -> r p o", p=128)
            ternD_w = ternD_d.rearrange("(og t p) m -> og p t m", t=8, p=128)
            wd_slab_free = {0: GC.n, 1: GC.n}
            wd_pn_free = {0: TC.n, 1: TC.n}
            wd_stage_free = {0: GW.n, 1: GW.n}
            wd_evmk = {}
            wd_jobs = [(r, og) for r in range(16) for og in range(8)]

            def wd_tern(i):
                r, og = wd_jobs[i]
                sidx = i % 2
                sb = slab[sidx]
                if i < 2:
                    sync.wait_ge(GC.h, slab_free[sidx])
                else:
                    sync.wait_ge(GC.h, wd_slab_free[sidx])
                lv = lds(SSL[sidx], sb, wd_r[r, :, og * 1024:(og + 1) * 1024])
                scalar.wait_ge(SSL[sidx].h, lv)
                scalar.wait_ge(TC.h, wd_pn_free[sidx])
                scalar.activation(pnh[sidx], sb, ACT.Sign,
                                  scale=wspos[:, 2:3], bias=bm05[:, 0:1])
                smark = SC.inc(scalar.activation(sb, sb, ACT.Sign,
                                                 scale=wspos[:, 2:3], bias=bp05[:, 0:1]))
                gpsimd.wait_ge(SC.h, smark)
                gpsimd.wait_ge(TC.h, wd_pn_free[sidx])
                wd_slab_free[sidx] = GC.inc(
                    gpsimd.tensor_tensor(pnh[sidx], pnh[sidx], sb, ALU.add))
                tensor.wait_ge(GC.h, wd_slab_free[sidx])
                marks = []
                for k8 in range(2):
                    bank = 4 + (i % 2) * 2 + k8
                    if bank in wd_evmk:
                        tensor.wait_ge(SC.h, wd_evmk[bank])
                    else:
                        if bank in evac_mark:
                            tensor.wait_ge(VC.h, evac_mark[bank])
                        if bank in h128ev:
                            tensor.wait_ge(VC.h, h128ev[bank])
                    for kk in range(4):
                        k = k8 * 4 + kk
                        mm = tensor.transpose(pp[bank][:, kk * 128:(kk + 1) * 128],
                                              pnh[sidx][:, k * 128:(k + 1) * 128],
                                              identf)
                    marks.append(TC.inc(mm))
                wd_pn_free[sidx] = marks[-1]
                return marks

            def wd_evac(i, marks):
                r, og = wd_jobs[i]
                sidx = i % 2
                stg = 4096 + sidx * 1024
                scalar.wait_ge(GW.h, wd_stage_free[sidx])
                for k8 in range(2):
                    bank = 4 + (i % 2) * 2 + k8
                    scalar.wait_ge(TC.h, marks[k8])
                    wd_evmk[bank] = SC.inc(scalar.copy(
                        wtb[:, stg + k8 * 512: stg + (k8 + 1) * 512], pp[bank]))
                gpsimd.wait_ge(SC.h, wd_evmk[bank])
                wd_stage_free[sidx] = gsts(
                    GW, ternD_w[og][:, :, r * 128:(r + 1) * 128],
                    wtb[:, stg:stg + 1024].rearrange("p (t m) -> p t m", m=128))

            wd_pend = None
            for i in range(128):
                marks = wd_tern(i)
                if wd_pend is not None:
                    wd_evac(*wd_pend)
                wd_pend = (i, marks)
            wd_evac(*wd_pend)
        if debug and hf == 0:
            gpsimd.wait_ge(VC.h, VC.n)
            for a in range(NA):
                gst(dbg["dbg_int"][:, a * 128:(a + 1) * 128],
                    inter[:, a * HALF: a * HALF + 128])
            vector.wait_ge(G.h, G.n)

        for s in [1, 2, 4, 8, 16, 32]:
            if s == 1:
                for a in range(NA):
                    if a & s:
                        continue
                    b2 = a + s
                    Aap = inter[:, a * HALF:(a + 1) * HALF]
                    Bap = inter[:, b2 * HALF:(b2 + 1) * HALF]
                    vector.tensor_tensor(Aap, Aap, Bap, ALU.add)
                    vector.scalar_tensor_tensor(Bap, Bap, -2.0, Aap, ALU.mult, ALU.add)
            else:
                for a in range(0, NA, 2):
                    if a & s:
                        continue
                    b2 = a + s
                    Aap = inter[:, a * HALF:(a + 2) * HALF]
                    Bap = inter[:, b2 * HALF:(b2 + 2) * HALF]
                    vector.tensor_tensor(Aap, Aap, Bap, ALU.add)
                    vector.scalar_tensor_tensor(Bap, Bap, -2.0, Aap, ALU.mult, ALU.add)
        mminb = dqb[:, 1024:1536]
        vector.tensor_copy(mmaxb, inter[:, 0:HALF])
        vector.tensor_copy(mminb, inter[:, 0:HALF])
        for a in range(1, NA):
            vector.tensor_tensor(mmaxb, mmaxb, inter[:, a * HALF:(a + 1) * HALF], ALU.max)
            vector.tensor_tensor(mminb, mminb, inter[:, a * HALF:(a + 1) * HALF], ALU.min)
        VC.inc(vector.engine_nop())
        tensor.wait_ge(VC.h, VC.n)
        tensor.wait_ge(SC.h, SC.n)
        for j in range(NHT):
            mm = tensor.transpose(pp[6][:, j * 128:(j + 1) * 128],
                                  mmaxb[:, j * 128:(j + 1) * 128], identf)
            mm = tensor.transpose(pp[7][:, j * 128:(j + 1) * 128],
                                  mminb[:, j * 128:(j + 1) * 128], identf)
        TC.inc(mm)
        vector.wait_ge(TC.h, TC.n)
        vector.tensor_reduce(sm[:, 8:12], pp[6].rearrange("p (j n) -> p j n", n=128),
                             AX.X, ALU.max)
        VC.inc(vector.tensor_reduce(sm[:, 4:8], pp[7].rearrange("p (j n) -> p j n", n=128),
                                    AX.X, ALU.min))
        vector.wait_ge(VC.h, VC.n)
        vector.tensor_scalar(sm[:, 4:8], sm[:, 4:8], -1.0, None, ALU.mult)
        VC.inc(vector.engine_nop())
        vector.wait_ge(VC.h, VC.n)
        VC.inc(vector.tensor_tensor(sm[:, 8:12], sm[:, 8:12], sm[:, 4:8], ALU.max))
        gpsimd.wait_ge(VC.h, VC.n)
        GC.inc(gpsimd.tensor_scalar(amax2, sm[:, 8:12], 1e-5, None, ALU.max))
        vector.wait_ge(GC.h, GC.n)
        VC.inc(vector.reciprocal(sm[:, 8:12], amax2))
        scalar.wait_ge(VC.h, VC.n)
        SC.inc(scalar.copy(sm[:, 12:16], sm[:, 8:12]))
        vector.wait_ge(SC.h, SC.n)
        VC.inc(vector.tensor_scalar(sm[:, 12:16], sm[:, 12:16], 128.0, None, ALU.mult))
        vector.tensor_scalar(deq2, amax2, wmb[:, 2:3], 1.0 / 256.0, ALU.mult, ALU.mult)
        VC.inc(vector.engine_nop())
        gpsimd.wait_ge(VC.h, VC.n)
        g1 = GA.inc(gpsimd.dma_start(out=s2row_d.rearrange("o (c p) -> o p c", p=128)[0],
                                     in_=sm[:, 12:16], allow_slow_non_contiguous=True), 16)
        sync.wait_ge(GA.h, g1)
        lds(SDQ, s2bb, s2row_d.broadcast_to((128, HALF)))
        lv = lds(SDQ, cbuf[:, 2560:3072], s2row_d.broadcast_to((128, HALF)))
        vector.wait_ge(SDQ.h, lv)
        VC.inc(vector.engine_nop())
        s2bbW = cbuf[:, 2048:3072]
        d2w = pnb[:, 0:1024]
        mk2w = pnb[:, 1024:2048]

        def quantize_pair(a):
            pair = inter[:, a * HALF:(a + 2) * HALF]
            vector.tensor_tensor(pair, pair, s2bbW, ALU.mult)
            vector.tensor_scalar(pair, pair, 128.5, 255.4999, ALU.add, ALU.min)
            vector.tensor_copy(d2w, pair)
            for bit in SEARCH_BITS:
                vector.tensor_scalar(mk2w, d2w, bit, -bit, ALU.is_ge, ALU.mult)
                vector.tensor_tensor(d2w, d2w, mk2w, ALU.add)
            return VC.inc(vector.scalar_tensor_tensor(pair, pair, 128.0, d2w,
                                                      ALU.subtract, ALU.subtract))


        xq2b = wtb[:, 2048:2048 + 2 * 512]
        wds = wtb[:, 0:2 * 1024]
        barrier()
        cmb_marks = {}
        out_marks = {}
        cmb_ji = 0
        for p2 in range(2):
            m0 = p2 * 1024
            dn_mm = {0: TC.n, 1: TC.n}
            for a in range(NA):
                ab2 = a % 2
                if a >= 2:
                    sync.wait_ge(TC.h, dn_mm[ab2])
                lv = lds(SWD[ab2], wds[:, ab2 * 1024:(ab2 + 1) * 1024].rearrange(
                    "p (h n) -> p h n", h=1)[:, 0, :],
                    ternD_d[a * 128:(a + 1) * 128, m0:m0 + 1024])
                if p2 == 0 and a % 2 == 0:
                    quantize_pair(a)
                if a >= 2:
                    vector.wait_ge(TC.h, dn_mm[ab2])
                VC.inc(vector.tensor_copy(xq2b[:, ab2 * 512:(ab2 + 1) * 512],
                                          inter[:, a * HALF:(a + 1) * HALF]))
                tensor.wait_ge(SWD[ab2].h, lv)
                tensor.wait_ge(VC.h, VC.n)
                for tg in range(NHT):
                    for m2 in range(2):
                        mm = tensor.matmul(
                            pp[tg * 2 + m2],
                            xq2b[:, ab2 * 512 + tg * 128: ab2 * 512 + (tg + 1) * 128],
                            wds[:, ab2 * 1024 + m2 * 512: ab2 * 1024 + (m2 + 1) * 512],
                            start=(a == 0), stop=(a == NA - 1))
                dn_mm[ab2] = TC.inc(mm)
            scalar.wait_ge(TC.h, TC.n)
            sync.wait_ge(TC.h, TC.n)
            csetA = (gt1, gt2, gt3)
            csetB = (pnb[:, 0:512], cbuf[:, 2048:2560], cbuf[:, 2560:3072])
            csets = [csetA, csetB]
            cmb_jobs = [(tg, m2) for tg in range(NHT) for m2 in range(2)]
            for (tg, m2) in cmb_jobs:
                ji = cmb_ji
                cmb_ji += 1
                par = ji % 2
                cg1, cg2, cg3 = csets[par]
                mc = m0 + m2 * 512
                rowa = t0 + tg * 128
                if ji >= 2:
                    sync.wait_ge(VC.h, cmb_marks[ji - 2])
                lds(SCMB[par], cg2, x_d[rowa:rowa + 128, mc:mc + 512])
                ls = lds(SCMB[par], cg3, s_sp[rowa:rowa + 128, mc:mc + 512])
                if ji >= 2:
                    scalar.wait_ge(GOUT.h, out_marks[ji - 2])
                SC.inc(scalar.activation(cg1, pp[tg * 2 + m2], ACT.Silu,
                                         scale=deq2[:, tg:tg + 1]))
                vector.wait_ge(SC.h, SC.n)
                vector.wait_ge(SCMB[par].h, ls)
                vector.tensor_tensor(cg1, cg1, cg3, ALU.mult)
                cmb_marks[ji] = VC.inc(vector.tensor_tensor(cg1, cg1, cg2, ALU.add))
                gpsimd.wait_ge(VC.h, VC.n)
                out_marks[ji] = gsts(GOUT, out_d[rowa:rowa + 128, mc:mc + 512], cg1)

    if debug:
        barrier()
        for i in range(NT):
            for mb2 in range(4):
                lv = ld(gt2, s_sp[i * 128:(i + 1) * 128, mb2 * 512:(mb2 + 1) * 512])
                gpsimd.wait_ge(LD.h, lv)
                gm = gst(dbg["dbg_s2"][i * 128:(i + 1) * 128, mb2 * 512:(mb2 + 1) * 512], gt2)
                sync.wait_ge(G.h, gm)
    barrier()
    dq_t.__exit__(None, None, None)
    wt_t.__exit__(None, None, None)
    cbuf_t.__exit__(None, None, None)
    inter_t.__exit__(None, None, None)
    return nc, dbg


def _host_constants():
    identf = np.eye(128, dtype=np.float32)
    identb = np.eye(128, dtype=np.float32).astype(ml_dtypes.bfloat16)
    h = np.array([[1.0]], dtype=np.float64)
    while h.shape[0] < 128:
        h = np.block([[h, h], [h, -h]])
    c = np.float32(1.0 / np.sqrt(np.float64(8192.0)))
    h128 = (h.astype(np.float32) * c).astype(np.float32)
    m1 = np.full((128, 128), -1e9, dtype=np.float32)
    for t8 in range(8):
        for i in range(16):
            for j in range(i + 1):
                m1[t8 * 16 + i, t8 * 16 + j] = 0.0
    mask4 = np.concatenate([m1] * 4, axis=1)
    ones = np.ones((128, 128), dtype=np.float32)
    return identf, identb, h128, mask4, ones


def make_in_maps(inputs):
    x = np.ascontiguousarray(np.asarray(inputs["L0_attn_x"], dtype=np.float32).reshape(TOK, D))
    wqkv = np.ascontiguousarray(np.asarray(inputs["L0_attn_Wqkv"], dtype=np.float32).reshape(D, 6144))
    bq = np.ascontiguousarray(
        np.asarray(inputs["L0_attn_bqkv"], dtype=np.float32).reshape(6144).reshape(48, 128).T)
    wout = np.ascontiguousarray(np.asarray(inputs["L0_attn_Wout"], dtype=np.float32).reshape(D, D))
    bout = np.ascontiguousarray(np.asarray(inputs["L0_attn_bout"], dtype=np.float32).reshape(1, D))
    wg = np.ascontiguousarray(np.asarray(inputs["L0_ffn_w_gate"], dtype=np.float32))
    wu = np.ascontiguousarray(np.asarray(inputs["L0_ffn_w_up"], dtype=np.float32))
    wd = np.ascontiguousarray(np.asarray(inputs["L0_ffn_w_down"], dtype=np.float32))
    identf, identb, h128, mask4, ones = _host_constants()
    maps = []
    for c in range(NCORES):
        maps.append({
            "x": x[c * TPC:(c + 1) * TPC], "wqkv": wqkv, "bqkv": bq, "wout": wout,
            "bout": bout, "wg": wg, "wu": wu, "wd": wd, "identf": identf,
            "h128": h128, "mask4": mask4, "ones": ones,
        })
    return maps


def kernel(L0_attn_x, L0_attn_Wqkv, L0_attn_bqkv, L0_attn_Wout, L0_attn_bout,
           L0_ffn_w_gate, L0_ffn_w_up, L0_ffn_w_down):
    maps = make_in_maps(dict(
        L0_attn_x=L0_attn_x, L0_attn_Wqkv=L0_attn_Wqkv, L0_attn_bqkv=L0_attn_bqkv,
        L0_attn_Wout=L0_attn_Wout, L0_attn_bout=L0_attn_bout,
        L0_ffn_w_gate=L0_ffn_w_gate, L0_ffn_w_up=L0_ffn_w_up,
        L0_ffn_w_down=L0_ffn_w_down))
    nc, _ = build_nc(debug=False, upto="C")
    res = run_bass_kernel_spmd(nc, maps, core_ids=list(range(NCORES)))
    out = np.concatenate([np.asarray(r["out"]) for r in res.results], axis=0)
    return out.reshape(4, 2048, D)
